# version 05 (final)
# baseline (speedup 1.0000x reference)
import numpy as np
import ml_dtypes
import concourse.bass as bass
import concourse.mybir as mybir
from concourse.bass_utils import run_bass_kernel_spmd

F32 = mybir.dt.float32
BF16 = mybir.dt.bfloat16
AF = mybir.ActivationFunctionType
ALU = mybir.AluOpType
AX = mybir.AxisListType

S = 4096
D = 1024
NT = 8
TT = 512
CH = 4
KC = 8
NCORES = 8

ENGINES = ("sp", "pe", "act", "dve", "pool")


class Plan:

    def __init__(self):
        self.q = {e: [] for e in ENGINES}
        self.cnt = {}
        self.sem_names = []
        self._touched = {e: False for e in ENGINES}

    def _sem(self, name):
        if name not in self.cnt:
            self.cnt[name] = 0
            self.sem_names.append(name)

    def add(self, eng, fn, waits=(), inc=None):
        waits = tuple((s, v) for (s, v) in waits if v > 0)
        for s, _ in waits:
            self._sem(s)
        if inc is not None:
            self._sem(inc[0])
            self.cnt[inc[0]] += inc[1]
        self.q[eng].append([waits, fn, inc])
        self._touched[eng] = True
        return self.cnt[inc[0]] if inc is not None else None

    def tok(self, sem):
        self._sem(sem)
        return (sem, self.cnt[sem])

    def barrier(self):
        self._sem("ph")
        n = 0
        for e in ENGINES:
            if not self._touched[e]:
                continue
            tail = self.q[e][-1] if self.q[e] else None
            if tail is not None and tail[2] is None:
                tail[2] = ("ph", 1)
            else:
                self.q[e].append([(), lambda eng: eng.nop(), ("ph", 1)])
            self.cnt["ph"] += 1
            n += 1
        waits = tuple((s, self.cnt[s]) for s in self.sem_names if self.cnt[s] > 0)
        for e in ENGINES:
            self.q[e].append([waits, None, None])
            self._touched[e] = False


def replay(nc, plan, sems):
    last_wait = {e: {} for e in ENGINES}

    def runner_for(eng_name):
        def run(engine):
            lw = last_wait[eng_name]
            for waits, fn, inc in plan.q[eng_name]:
                for s, v in waits:
                    if lw.get(s, 0) >= v:
                        continue
                    engine.wait_ge(sems[s], v)
                    lw[s] = v
                if fn is None:
                    continue
                inst = fn(engine)
                if inc is not None:
                    inst.then_inc(sems[inc[0]], inc[1])
        return run

    with nc.Block() as block:
        block.sync(runner_for("sp"))
        block.tensor(runner_for("pe"))
        block.scalar(runner_for("act"))
        block.vector(runner_for("dve"))
        block.gpsimd(runner_for("pool"))



def host_prepare(inputs):
    f32 = np.float32
    bf16 = ml_dtypes.bfloat16
    sh = {}
    sh["c_ident"] = np.eye(128, dtype=bf16)
    sh["c_identf"] = np.eye(128, dtype=f32)
    sh["c_ones1"] = np.ones((1, 128), dtype=f32)
    kk = np.arange(128)[:, None]
    qq = np.arange(128)[None, :]
    sh["c_triu"] = (qq <= kk).astype(bf16)
    sh["c_trils"] = (qq > kk).astype(bf16)
    lat = np.zeros((64, 32, 128), dtype=bf16)
    l = np.arange(64)[:, None]
    for c in range(32):
        lim = 2 * c + (np.arange(128)[None, :] >= 64).astype(np.int64)
        lat[:, c, :] = (l < lim).astype(bf16)
    sh["c_latmask"] = lat
    pc = np.zeros((128, 32, 64), dtype=bf16)
    for c in range(32):
        for half in range(2):
            pc[half * 64:(half + 1) * 64, c, 2 * c + half] = bf16(1.0 / 64.0)
    sh["c_pool"] = pc
    zv = np.ones((128, 1), dtype=f32); zv[0] = 0.0
    sh["c_zvec1"] = zv
    zv64 = np.ones((128, 1), dtype=f32); zv64[:64] = 0.0
    sh["c_zvec64"] = zv64

    sh["A_qkvw"] = inputs["L0_attn_qkv_w"]
    sh["A_qkvb"] = np.ascontiguousarray(inputs["L0_attn_qkv_b"].reshape(24, 128).T)
    sh["A_rqw"] = inputs["L0_attn_rq_w"]
    sh["A_rqb"] = np.ascontiguousarray(inputs["L0_attn_rq_b"].reshape(2, 128).T)
    sh["A_rkvw"] = inputs["L0_attn_rkv_w"]
    sh["A_rkvb"] = np.ascontiguousarray(inputs["L0_attn_rkv_b"].reshape(4, 128).T)
    sh["A_routw"] = inputs["L0_attn_rout_w"]
    sh["A_routb"] = inputs["L0_attn_rout_b"]
    sh["A_outw"] = inputs["L0_attn_out_w"]
    sh["A_outb"] = inputs["L0_attn_out_b"]
    sh["A_glw"] = np.ascontiguousarray(inputs["L0_attn_gl_w"].reshape(1, 64))
    sh["A_grw"] = np.ascontiguousarray(inputs["L0_attn_gr_w"].reshape(1, 64))
    sh["B_gwT"] = np.ascontiguousarray(inputs["L0_ffn_gate_w"].T)
    sh["B_w1"] = np.ascontiguousarray(inputs["L0_ffn_w1"].transpose(2, 1, 0))
    sh["B_b1"] = np.ascontiguousarray(
        inputs["L0_ffn_b1"].reshape(8, 32, 128).transpose(0, 2, 1))
    sh["B_w2"] = np.ascontiguousarray(inputs["L0_ffn_w2"].transpose(2, 1, 0))
    sh["B_b2"] = inputs["L0_ffn_b2"]
    sh["C_wq"] = inputs["L1_attn_W_Q"]
    sh["C_wk"] = inputs["L1_attn_W_K"]
    sh["C_wv"] = inputs["L1_attn_W_V"]
    sh["D_dwk"] = np.ascontiguousarray(inputs["L1_ffn_dw_kernel"].reshape(1024, 3))
    sh["D_dbf"] = np.ascontiguousarray(inputs["L1_ffn_dw_bias"].reshape(8, 128).T)
    sh["D_pww"] = np.ascontiguousarray(inputs["L1_ffn_pw_w"].T)
    sh["D_pwbf"] = np.ascontiguousarray(inputs["L1_ffn_pw_b"].reshape(8, 128).T)
    sh["D_rwT"] = np.ascontiguousarray(inputs["L1_ffn_router_w"].T)
    sh["D_rb"] = np.ascontiguousarray(inputs["L1_ffn_router_b"].reshape(1, 8))
    sh["D_ewT"] = np.ascontiguousarray(inputs["L1_ffn_expert_w"].transpose(0, 2, 1))
    sh["D_ebrow"] = np.ascontiguousarray(inputs["L1_ffn_expert_b"].reshape(1, 8192))
    sh["D_owT"] = np.ascontiguousarray(inputs["L1_ffn_out_w"].T)
    sh["D_ob"] = np.ascontiguousarray(inputs["L1_ffn_out_b"].reshape(1, 1024))
    sh["D_nw"] = np.ascontiguousarray(inputs["L1_ffn_norm_w"].reshape(1, 1024))
    sh["D_ffdT"] = np.ascontiguousarray(inputs["L1_ffn_ff_down_w"].T)
    sh["D_ffdbf"] = np.ascontiguousarray(inputs["L1_ffn_ff_down_b"].reshape(2, 128).T)
    sh["D_ffuT"] = np.ascontiguousarray(inputs["L1_ffn_ff_up_w"].T)
    sh["D_ffub"] = np.ascontiguousarray(inputs["L1_ffn_ff_up_b"].reshape(1, 1024))
    sh["A_vb_row"] = np.ascontiguousarray(inputs["L0_attn_qkv_b"][2048:].reshape(1, 1024))
    sh["A_rvb_row"] = np.ascontiguousarray(inputs["L0_attn_rkv_b"][256:].reshape(1, 256))
    sh["A_routb_row"] = np.ascontiguousarray(inputs["L0_attn_rout_b"].reshape(1, 1024))
    sh["A_outb_row"] = np.ascontiguousarray(inputs["L0_attn_out_b"].reshape(1, 1024))

    xs = [np.ascontiguousarray(inputs["L0_attn_x"][c]) for c in range(NCORES)]
    return sh, xs



class G:

    def __init__(self, nc, stack):
        self.nc = nc
        self.stack = stack
        self.plan = Plan()
        self.H = {}
        self.pb = [stack.enter_context(nc.psum_tensor(f"pb{i}", [128, 512], F32)).ap()
                   for i in range(8)]
        self.bankfree = [("ph", 0)] * 8
        self.consts = {}
        self.gate_sb = stack.enter_context(nc.sbuf_tensor("gate_sb", [1, 8], F32)).ap()

    def dram(self, name, shape, dtype, kind):
        if name in self.H:
            return self.H[name]
        t = self.nc.dram_tensor(name, list(shape), dtype, kind=kind)
        self.H[name] = t.ap()
        return self.H[name]

    def sb(self, name, shape, dtype):
        return self.stack.enter_context(self.nc.sbuf_tensor(name, list(shape), dtype)).ap()

    def load(self, dst, src, sem="ld", q="sp", waits=()):
        return self.plan.add(q, lambda e: e.dma_start(out=dst, in_=src),
                             waits=waits, inc=(sem, 16))

    def loadc(self, dst, src, sem="ld", waits=()):
        return self.plan.add("pool", lambda e: e.dma_start(out=dst, in_=src),
                             waits=waits, inc=(sem, 16))

    def gate_loads(self):
        return self.plan.tok("ld")

    def store(self, dst, src, sem="st", q="sp", waits=()):
        return self.plan.add(q, lambda e: e.dma_start(out=dst, in_=src),
                             waits=waits, inc=(sem, 16))

    def accum(self, dst, src, sem="st", waits=()):
        return self.plan.add("pool",
                             lambda e: e.dma_start(out=dst, in_=src, accum_op=ALU.add),
                             waits=waits, inc=(sem, 16))

    def mm(self, bank, pairs, waits=(), out=None, n0=0, n1=None):
        nk = len(pairs)
        allw = tuple(waits) + (self.bankfree[bank],)
        for i, (lh, rh) in enumerate(pairs):
            o = out if out is not None else self.pb[bank][:lh.shape[-1], :rh.shape[-1]]
            tok = self.plan.add(
                "pe",
                (lambda lh=lh, rh=rh, o=o, i=i: (
                    lambda e: e.matmul(o, lh, rh, start=(i == 0), stop=(i == nk - 1))))(),
                waits=allw if i == 0 else (),
                inc=("pe", 1) if i == nk - 1 else None)
        return ("pe", tok)

    def transpose4(self, bank, srcs, dsts, waits=(), evac_eng="act"):
        n = len(srcs)
        allw = tuple(waits) + (self.bankfree[bank],) + \
            ((self.consts["ident_tok"],) if "ident_tok" in self.consts else ())
        for j, s in enumerate(srcs):
            o = self.pb[bank][:, j * 128:(j + 1) * 128]
            self.plan.add(
                "pe",
                (lambda s=s, o=o: (lambda e: e.matmul(
                    o, s, self.consts["ident"], start=True, stop=True)))(),
                waits=allw if j == 0 else (),
                inc=("pe", 1) if j == n - 1 else None)
        petok = self.plan.tok("pe")
        etok = None
        for j, d in enumerate(dsts):
            src = self.pb[bank][:, j * 128:(j + 1) * 128]
            if evac_eng == "act":
                etok = self.plan.add(
                    "act",
                    (lambda d=d, src=src: (lambda e: e.copy(d, src)))(),
                    waits=(("pe", petok[1]),) if j == 0 else (),
                    inc=("act", 1) if j == n - 1 else None)
                etok = ("act", self.plan.cnt["act"])
            else:
                etok = self.plan.add(
                    "dve",
                    (lambda d=d, src=src: (lambda e: e.tensor_copy(d, src)))(),
                    waits=(("pe", petok[1]),) if j == 0 else (),
                    inc=("dve", 1) if j == n - 1 else None)
                etok = ("dve", self.plan.cnt["dve"])
        self.bankfree[bank] = etok
        return etok

    def bias_bcast(self, dst, row_sb, width, waits=()):
        toks = []
        off = 0
        while off < width:
            w = min(512, width - off)
            bank = self._next_bcast_bank()
            allw = tuple(waits) + (self.bankfree[bank],)
            self.plan.add(
                "pe",
                (lambda bank=bank, off=off, w=w: (lambda e: e.matmul(
                    self.pb[bank][:, :w], self.consts["ones1"],
                    row_sb[:, off:off + w], start=True, stop=True)))(),
                waits=allw, inc=("pe", 1))
            petok = self.plan.tok("pe")
            t = self.plan.add(
                "dve",
                (lambda bank=bank, off=off, w=w: (lambda e: e.tensor_copy(
                    dst[:, off:off + w], self.pb[bank][:, :w])))(),
                waits=(petok,), inc=("dve", 1))
            self.bankfree[bank] = ("dve", t)
            toks.append(("dve", t))
            off += w
        return toks[-1]

    _bcast_rr = 0

    def _next_bcast_bank(self):
        self._bcast_rr = (self._bcast_rr + 1) % 6
        return self._bcast_rr


def _dv(g, fn, waits=()):
    return ("dve", g.plan.add("dve", fn, waits=waits, inc=("dve", 1)))


def _ac(g, fn, waits=()):
    return ("act", g.plan.add("act", fn, waits=waits, inc=("act", 1)))


def emit_consts(g, which=("ident", "ones1")):
    toks = []
    if "ident" in which:
        g.consts["ident"] = g.sb("ident_sb", [128, 128], BF16)
        g.consts["ident_tok"] = ("ld", g.load(g.consts["ident"], g.H["c_ident"]))
        toks.append(g.consts["ident_tok"])
    if "ones1" in which:
        g.consts["ones1"] = g.sb("ones1_sb", [1, 128], F32)
        g.consts["ones1_tok"] = ("ld", g.load(g.consts["ones1"], g.H["c_ones1"]))
        toks.append(g.consts["ones1_tok"])
    return toks


def emit_ln_tiles(g, name, src_dram, t, bufs, waits_extra=()):
    p = g.plan
    fr = bufs["free"]
    xb = bufs["sbx"][t % 2]
    src = src_dram.rearrange("(n c p) d -> n p c d", p=128, c=CH)[t]
    xtok = ("ld", g.load(xb, src, waits=(fr.get(("sbx", t % 2), ("ph", 0)),) + tuple(waits_extra)))
    t1 = p.add("dve", lambda e, xb=xb: e.reduce_sum(bufs["sums"], xb, axis=AX.X),
               waits=(xtok,), inc=("dve", 1))
    if bufs.get("sq") is not None:
        sqtok_w = fr.get(("sq",), ("ph", 0))
        atok = p.add("act", lambda e, xb=xb: e.square(bufs["sq"], xb.rearrange("p c d -> p (c d)")),
                     waits=(xtok, sqtok_w), inc=("act", 1))
        t2 = p.add("dve", lambda e: e.reduce_sum(bufs["sumsq"], bufs["sq"].rearrange("p (c d) -> p c d", c=CH), axis=AX.X),
                   waits=(("act", atok),), inc=("dve", 1))
        fr[("sq",)] = ("dve", t2)
        t2 = ("dve", t2)
    else:
        atok = None
        for c in range(CH):
            atok = p.add(
                "act",
                (lambda c=c, xb=xb: (lambda e: e.activation(
                    bufs["lntm"][:, c, :], xb[:, c, :], AF.Square,
                    accum_out=bufs["sumsq"][:, c:c + 1])))(),
                waits=(xtok, fr.get(("lntm",), ("ph", 0))) if c == 0 else (),
                inc=("act", 1))
        t2 = ("act", atok)
    t3 = p.add("dve", lambda e: e.tensor_scalar(bufs["mean"], bufs["sums"], 1.0 / D, None, ALU.mult),
               waits=(("dve", t1),), inc=("dve", 1))
    t4 = p.add("dve", lambda e: e.tensor_scalar(bufs["msq"], bufs["sumsq"], 1.0 / D, 1e-5,
                                                ALU.mult, ALU.add),
               waits=(t2,), inc=("dve", 1))
    t5 = p.add("act", lambda e: e.square(bufs["m2"], bufs["mean"]),
               waits=(("dve", t3),), inc=("act", 1))
    t6 = p.add("dve", lambda e: e.tensor_tensor(bufs["var"], bufs["msq"], bufs["m2"], ALU.subtract),
               waits=(("act", t5), ("dve", t4)), inc=("dve", 1))
    stok = p.add("act", lambda e: e.activation(bufs["std"], bufs["var"], AF.Sqrt),
                 waits=(("dve", t6),), inc=("act", 1))
    rtok = p.add("dve", lambda e: e.reciprocal(bufs["rstd"], bufs["std"]),
                 waits=(("act", stok),), inc=("dve", 1))
    lnw = (("dve", rtok), fr.get(("lntm",), ("ph", 0)))
    lntok = None
    for c in range(CH):
        lntok = p.add(
            "dve",
            (lambda c=c, xb=xb: (lambda e: e.tensor_scalar(
                bufs["lntm"][:, c, :], xb[:, c, :],
                bufs["mean"][:, c:c + 1], bufs["rstd"][:, c:c + 1],
                ALU.subtract, ALU.mult)))(),
            waits=lnw if c == 0 else (),
            inc=("dve", 1) if c == CH - 1 else None)
    fr[("sbx", t % 2)] = ("dve", lntok)
    return xtok, ("dve", lntok)


def emit_tp_to_fm(g, lntm, lnfm, lntok, banks=(0, 1), fr=None, key=None, col_off=0):
    etok = None
    bi = 0
    for c in range(CH):
        for kg in range(2):
            srcs = [lntm[:, c, (kg * 4 + j) * 128:(kg * 4 + j + 1) * 128] for j in range(4)]
            dsts = [lnfm[:, kg * 4 + j, col_off + c * 128:col_off + (c + 1) * 128]
                    for j in range(4)]
            w = [lntok]
            if fr is not None and key is not None:
                w.append(fr.get(key, ("ph", 0)))
            etok = g.transpose4(banks[bi % len(banks)], srcs, dsts, waits=tuple(w))
            bi += 1
    return etok



def build_phase_C(g, rB_name="rB", rC_name="rC"):
    nc, p = g.nc, g.plan
    rB = g.H[rB_name]
    rC = g.H[rC_name]
    qfmC = g.dram("qfmC", [KC, 128, S], BF16, "Internal")
    kfmC = g.dram("kfmC", [KC, 128, S], BF16, "Internal")
    vaugC = g.dram("vaugC", [32, 128, 1025], BF16, "Internal")

    from contextlib import ExitStack
    with ExitStack() as st:
        sb = lambda n, shp, dt: st.enter_context(nc.sbuf_tensor(n, list(shp), dt)).ap()
        build_phase_C1(g, sb, rB, qfmC, kfmC, vaugC)
    p.barrier()
    with ExitStack() as st:
        sb = lambda n, shp, dt: st.enter_context(nc.sbuf_tensor(n, list(shp), dt)).ap()
        build_phase_C2(g, sb, rB, rC, qfmC, kfmC, vaugC, rC_name)
    p.barrier()


def build_phase_C1(g, sb, rB, qfmC, kfmC, vaugC):
    nc, p = g.nc, g.plan
    if True:
        wq = sb("C_wq_sb", [128, KC, 1024], BF16)
        wk = sb("C_wk_sb", [128, KC, 1024], BF16)
        wv = sb("C_wv_sb", [128, KC, 1024], BF16)
        wtok = [("ld", g.loadc(wq, g.H["C_wq"].rearrange("(c p) o -> p c o", p=128))),
                ("ld", g.loadc(wk, g.H["C_wk"].rearrange("(c p) o -> p c o", p=128))),
                ("ld", g.loadc(wv, g.H["C_wv"].rearrange("(c p) o -> p c o", p=128)))]
        wtok = [g.gate_loads()]
        bufs = {
            "sbx": [sb("C_sbx0", [128, CH, D], F32), sb("C_sbx1", [128, CH, D], F32)],
            "sq": sb("C_sq", [128, CH * D], F32),
            "sums": sb("C_sums", [128, CH], F32), "sumsq": sb("C_sumsq", [128, CH], F32),
            "mean": sb("C_mean", [128, CH], F32), "var": sb("C_var", [128, CH], F32),
            "msq": sb("C_msq", [128, CH], F32), "m2": sb("C_m2", [128, CH], F32),
            "std": sb("C_std", [128, CH], F32), "rstd": sb("C_rstd", [128, CH], F32),
            "lntm": sb("C_lntm", [128, CH, D], BF16),
            "free": {},
        }
        lnfm = sb("C_lnfm", [128, KC, TT], BF16)
        qt = sb("C_qt", [128, KC, TT], BF16)
        kt = sb("C_kt", [128, KC, TT], BF16)
        vt = sb("C_vt", [128, CH, 1025], BF16)
        fr = bufs["free"]
        for t in range(NT):
            xtok, lntok = emit_ln_tiles(g, "C", rB, t, bufs, waits_extra=tuple(wtok))
            if "dbg_lntm" in g.H:
                g.store(g.H["dbg_lntm"].rearrange("(n c p) d -> n p c d", p=128, c=CH)[t],
                        bufs["lntm"], waits=(lntok,))
            if "dbg_stat" in g.H:
                g.store(g.H["dbg_stat"][t], bufs["mean"], waits=(lntok,))
                g.store(g.H["dbg_stat2"][t], bufs["rstd"], waits=(lntok,))
            fmtok = emit_tp_to_fm(g, bufs["lntm"], lnfm, lntok, banks=(0, 1),
                                  fr=fr, key=("lnfm",))
            fr[("lntm",)] = g.plan.tok("pe")
            if "dbg_lnfm" in g.H:
                g.store(g.H["dbg_lnfm"][:, :, t * TT:(t + 1) * TT].rearrange("c p t -> p c t"),
                        lnfm, waits=(fmtok,))
            for which, wsb, dst, dram, scale in (
                    ("q", wq, qt, qfmC, 1.0 / 32.0), ("k", wk, kt, kfmC, 1.0)):
                for oc in range(KC):
                    bank = 2 + (oc % 2)
                    pairs = [(wsb[:, dc, oc * 128:(oc + 1) * 128], lnfm[:, dc, :])
                             for dc in range(KC)]
                    w = [fmtok] + wtok + [fr.get((which, oc % 2), ("ph", 0))]
                    petok = g.mm(bank, pairs, waits=w)
                    etok = p.add(
                        "act",
                        (lambda oc=oc, bank=bank, dst=dst, scale=scale: (
                            lambda e: e.activation(dst[:, oc, :], g.pb[bank][:, :TT],
                                                   AF.Identity, scale=scale)))(),
                        waits=(petok,), inc=("act", 1))
                    g.bankfree[bank] = ("act", etok)
                    fr[(which, oc % 2)] = ("act", etok)
                stok = g.store(dram[:, :, t * TT:(t + 1) * TT].rearrange("c p t -> p c t"),
                               dst, waits=(("act", etok),))
                fr[(which, 0)] = ("st", stok)
                fr[(which, 1)] = ("st", stok)
            mtok = p.add("dve", lambda e: e.memset(vt[:, :, 1024:1025], 1.0),
                         waits=(fr.get(("vt",), ("ph", 0)),), inc=("dve", 1))
            vetok = None
            for qc in range(CH):
                for half in range(2):
                    bank = 4 + ((qc * 2 + half) % 2)
                    pairs = [(lnfm[:, dc, qc * 128:(qc + 1) * 128],
                              wv[:, dc, half * 512:(half + 1) * 512]) for dc in range(KC)]
                    petok = g.mm(bank, pairs, waits=[fmtok] + wtok)
                    vetok = p.add(
                        "dve",
                        (lambda qc=qc, half=half, bank=bank: (
                            lambda e: e.tensor_copy(vt[:, qc, half * 512:(half + 1) * 512],
                                                    g.pb[bank][:, :512])))(),
                        waits=(petok, ("dve", mtok)), inc=("dve", 1))
                    g.bankfree[bank] = ("dve", vetok)
            stok = g.store(vaugC[t * CH:(t + 1) * CH].rearrange("c p v -> p c v"),
                           vt, waits=(("dve", vetok),))
            fr[("vt",)] = ("st", stok)
            fr[("lnfm",)] = g.plan.tok("pe")
            p.barrier()


def build_phase_C2(g, sb, rB, rC, qfmC, kfmC, vaugC, rC_name):
    nc, p = g.nc, g.plan
    if True:
        vall = sb("C2_vall", [128, 32, 1025], BF16)
        vltok = ("ld", g.load(vall, vaugC.rearrange("c p v -> p c v")))
        qbuf = [sb("C2_q0", [128, KC, TT], BF16), sb("C2_q1", [128, KC, TT], BF16)]
        kbuf = [sb("C2_k0", [128, KC, 256], BF16), sb("C2_k1", [128, KC, 256], BF16)]
        rbuf = [sb("C2_rb0", [128, CH, D], F32), sb("C2_rb1", [128, CH, D], F32)]
        obuf = [sb("C2_o0", [128, CH, D], F32), sb("C2_o1", [128, CH, D], F32)]
        et = sb("C2_et", [128, 32, TT], BF16)
        rz = sb("C2_rz", [128, 1], F32)
        fr2 = {}
        for qt_i in range(NT):
            qb = qbuf[qt_i % 2]
            rb = rbuf[qt_i % 2]
            ob = obuf[qt_i % 2]
            qtok = ("ld", g.load(qb, qfmC[:, :, qt_i * TT:(qt_i + 1) * TT]
                                 .rearrange("c p t -> p c t"),
                                 waits=(fr2.get(("q", qt_i % 2), ("ph", 0)), vltok)))
            rtok = ("ld", g.load(rb, rB.rearrange("(n c p) d -> n p c d", p=128, c=CH)[qt_i],
                                 waits=(fr2.get(("rb", qt_i % 2), ("ph", 0)), vltok)))
            etok_last = None
            for kc2 in range(16):
                kb = kbuf[kc2 % 2]
                ktok = ("ld", g.load(kb, kfmC[:, :, kc2 * 256:(kc2 + 1) * 256]
                                     .rearrange("c p t -> p c t"),
                                     waits=(fr2.get(("k", kc2 % 2), ("ph", 0)), vltok)))
                for j in range(2):
                    kc = kc2 * 2 + j
                    bank = 0 + (kc % 2)
                    pairs = [(kb[:, dc, j * 128:(j + 1) * 128], qb[:, dc, :])
                             for dc in range(KC)]
                    w = [ktok, qtok, fr2.get(("et",), ("ph", 0))]
                    petok = g.mm(bank, pairs, waits=w)
                    etok_last = p.add(
                        "act",
                        (lambda kc=kc, bank=bank: (
                            lambda e: e.activation(et[:, kc, :], g.pb[bank][:, :TT], AF.Exp)))(),
                        waits=(petok,), inc=("act", 1))
                    g.bankfree[bank] = ("act", etok_last)
                fr2[("k", kc2 % 2)] = g.plan.tok("pe")
            ettok = ("act", etok_last)
            last_store = None
            for qc in range(CH):
                bz = 4
                bu = (2, 3)
                zpairs = [(et[:, kc, qc * 128:(qc + 1) * 128], vall[:, kc, 1024:1025])
                          for kc in range(32)]
                upairs0 = [(et[:, kc, qc * 128:(qc + 1) * 128], vall[:, kc, 0:512])
                           for kc in range(32)]
                upairs1 = [(et[:, kc, qc * 128:(qc + 1) * 128], vall[:, kc, 512:1024])
                           for kc in range(32)]
                ztok = g.mm(bz, zpairs, waits=[ettok, vltok])
                u0tok = g.mm(bu[0], upairs0, waits=[ettok, vltok])
                u1tok = g.mm(bu[1], upairs1, waits=[ettok, vltok])
                rztok = p.add("dve", lambda e: e.reciprocal(rz, g.pb[bz][:, 0:1]),
                              waits=(ztok, ("dve", g.plan.cnt["dve"])), inc=("dve", 1))
                g.bankfree[bz] = ("dve", rztok)
                otok = None
                for half, utok, bank in ((0, u0tok, bu[0]), (1, u1tok, bu[1])):
                    otok = p.add(
                        "dve",
                        (lambda qc=qc, half=half, bank=bank, rb=rb, ob=ob: (
                            lambda e: e.scalar_tensor_tensor(
                                ob[:, qc, half * 512:(half + 1) * 512],
                                g.pb[bank][:, :512], rz,
                                rb[:, qc, half * 512:(half + 1) * 512],
                                ALU.mult, ALU.add)))(),
                        waits=(utok, rtok, ("dve", rztok),
                               fr2.get(("o", qt_i % 2), ("ph", 0))),
                        inc=("dve", 1))
                    g.bankfree[bank] = ("dve", otok)
            sttok = g.store(rC_ap(g, rC_name, qt_i), ob, waits=(("dve", otok),))
            fr2[("rb", qt_i % 2)] = ("dve", otok)
            fr2[("q", qt_i % 2)] = g.plan.tok("pe")
            fr2[("o", qt_i % 2)] = ("st", sttok)
            fr2[("et",)] = g.plan.tok("pe")
            p.barrier()


def rC_ap(g, name, t):
    return g.H[name].rearrange("(n c p) d -> n p c d", p=128, c=CH)[t]



def build_phase_A(g, x_name="x0", rA_name="rA"):
    nc, p = g.nc, g.plan
    x0 = g.H[x_name]
    qfmA = g.dram("qfmA", [KC, 128, S], BF16, "Internal")
    kfmA = g.dram("kfmA", [KC, 128, S], BF16, "Internal")
    vaugA = g.dram("vaugA", [32, 128, 16 * 65], BF16, "Internal")
    rqfmA = g.dram("rqfmA", [2, 128, S], BF16, "Internal")
    from contextlib import ExitStack
    with ExitStack() as st:
        sb = lambda n, shp, dt: st.enter_context(nc.sbuf_tensor(n, list(shp), dt)).ap()
        build_phase_A1(g, sb, x0, qfmA, kfmA, vaugA, rqfmA)
    p.barrier()
    with ExitStack() as st:
        sb = lambda n, shp, dt: st.enter_context(nc.sbuf_tensor(n, list(shp), dt)).ap()
        build_phase_A2(g, sb, x0, g.H[rA_name], qfmA, kfmA, vaugA, rqfmA)
    p.barrier()


def build_phase_A1(g, sb, x0, qfmA, kfmA, vaugA, rqfmA):
    nc, p = g.nc, g.plan
    qkvw = sb("A_qkvw_sb", [128, KC, 3072], BF16)
    rqw = sb("A_rqw_sb", [128, KC, 256], BF16)
    qkvb = sb("A_qkvb_sb", [128, 24], F32)
    qb4 = sb("A_qb4_sb", [128, 8], F32)
    rqb = sb("A_rqb_sb", [128, 2], F32)
    rqb4 = sb("A_rqb4_sb", [128, 2], F32)
    pconst = sb("A_pconst_sb", [128, 32, 64], BF16)
    wtok = [("ld", g.loadc(qkvw, g.H["A_qkvw"].rearrange("(c p) o -> p c o", p=128)))]
    wtok = [("ld", g.loadc(rqw, g.H["A_rqw"].rearrange("(c p) o -> p c o", p=128)))]
    btok = ("ld", g.load(qkvb, g.H["A_qkvb"]))
    g.load(rqb, g.H["A_rqb"])
    ptok = ("ld", g.load(pconst, g.H["c_pool"]))
    wtok = [g.gate_loads()]
    qb4tok = _dv(g, lambda e: e.tensor_scalar(qb4, qkvb[:, 0:8], 0.125, None, ALU.mult),
                 waits=(btok,))
    rqb4tok = _dv(g, lambda e: e.tensor_scalar(rqb4, rqb, 0.125, None, ALU.mult),
                  waits=(g.plan.tok("ld"),))
    bufs = {
        "sbx": [sb("A_sbx0", [128, CH, D], F32), sb("A_sbx1", [128, CH, D], F32)],
        "sq": sb("A_sq", [128, CH * D], F32),
        "sums": sb("A_sums", [128, CH], F32), "sumsq": sb("A_sumsq", [128, CH], F32),
        "mean": sb("A_mean", [128, CH], F32), "var": sb("A_var", [128, CH], F32),
        "msq": sb("A_msq", [128, CH], F32), "m2": sb("A_m2", [128, CH], F32),
        "std": sb("A_std", [128, CH], F32), "rstd": sb("A_rstd", [128, CH], F32),
        "lntm": sb("A_lntm", [128, CH, D], BF16),
        "free": {},
    }
    lnfm = sb("A_lnfm", [128, KC, TT], BF16)
    qt = sb("A_qt", [128, KC, TT], BF16)
    kt = sb("A_kt", [128, KC, TT], BF16)
    rqt = sb("A_rqt", [128, 2, TT], BF16)
    vt = sb("A_vt", [128, CH, 16 * 65], BF16)
    vt4 = vt.rearrange("p c (h v) -> p c h v", v=65)
    fr = bufs["free"]
    for t in range(NT):
        xtok, lntok = emit_ln_tiles(g, "A", x0, t, bufs, waits_extra=tuple(wtok))
        for c in range(CH):
            gc = t * CH + c
            for half2, bank in ((0, 6), (1, 7)):
                p.add("pe",
                      (lambda c=c, gc=gc, half2=half2, bank=bank: (
                          lambda e: e.matmul(
                              g.pb[bank][0:64, :],
                              pconst[:, gc, :],
                              bufs["lntm"][:, c, half2 * 512:(half2 + 1) * 512],
                              start=(gc == 0), stop=(gc == 31))))(),
                      waits=(lntok, ptok) if (c == 0 and half2 == 0) else (),
                      inc=("pe", 1) if (c == CH - 1 and half2 == 1) else None)
        fmtok = emit_tp_to_fm(g, bufs["lntm"], lnfm, lntok, banks=(0, 1),
                              fr=fr, key=("lnfm",))
        fr[("lntm",)] = g.plan.tok("pe")
        for which, ocbase, dst, dram, scale, btile, bcol0 in (
                ("q", 0, qt, qfmA, 0.125, qb4, 0),
                ("k", 8, kt, kfmA, 1.0, qkvb, 8)):
            etok = None
            for oc in range(KC):
                bank = 2 + (oc % 2)
                pairs = [(qkvw[:, dc, (ocbase + oc) * 128:(ocbase + oc + 1) * 128],
                          lnfm[:, dc, :]) for dc in range(KC)]
                w = [fmtok] + wtok + ([qb4tok] if which == "q" else [btok])
                petok = g.mm(bank, pairs, waits=w)
                etok = p.add(
                    "act",
                    (lambda oc=oc, bank=bank, dst=dst, scale=scale, btile=btile, bc=bcol0: (
                        lambda e: e.activation(dst[:, oc, :], g.pb[bank][:, :TT],
                                               AF.Identity, bias=btile[:, bc + oc:bc + oc + 1],
                                               scale=scale)))(),
                    waits=(petok, fr.get((which,), ("ph", 0))), inc=("act", 1))
                g.bankfree[bank] = ("act", etok)
            stok = g.store(dram[:, :, t * TT:(t + 1) * TT].rearrange("c p t -> p c t"),
                           dst, waits=(("act", etok),))
            fr[(which,)] = ("st", stok)
        etok = None
        for oc in range(2):
            bank = 2 + (oc % 2)
            pairs = [(rqw[:, dc, oc * 128:(oc + 1) * 128], lnfm[:, dc, :])
                     for dc in range(KC)]
            petok = g.mm(bank, pairs, waits=[fmtok] + wtok + [rqb4tok])
            etok = p.add(
                "act",
                (lambda oc=oc, bank=bank: (
                    lambda e: e.activation(rqt[:, oc, :], g.pb[bank][:, :TT],
                                           AF.Identity, bias=rqb4[:, oc:oc + 1],
                                           scale=0.125)))(),
                waits=(petok, fr.get(("rq",), ("ph", 0))), inc=("act", 1))
            g.bankfree[bank] = ("act", etok)
        stok = g.store(rqfmA[:, :, t * TT:(t + 1) * TT].rearrange("c p t -> p c t"),
                       rqt, waits=(("act", etok),))
        fr[("rq",)] = ("st", stok)
        mtok = _dv(g, lambda e: e.memset(vt4[:, :, :, 64:65], 1.0),
                   waits=(fr.get(("vt",), ("ph", 0)),))
        vtok = None
        for qc in range(CH):
            for half in range(2):
                bank = 4 + ((qc * 2 + half) % 2)
                pairs = [(lnfm[:, dc, qc * 128:(qc + 1) * 128],
                          qkvw[:, dc, 2048 + half * 512:2048 + (half + 1) * 512])
                         for dc in range(KC)]
                petok = g.mm(bank, pairs, waits=[fmtok] + wtok)
                vtok = _dv(g, (lambda qc=qc, half=half, bank=bank: (
                    lambda e: e.tensor_copy(
                        vt4[:, qc, half * 8:(half + 1) * 8, 0:64],
                        g.pb[bank][:, :512].rearrange("p (h v) -> p h v", v=64))))(),
                           waits=(petok, mtok))
                g.bankfree[bank] = vtok
        stok = g.store(vaugA[t * CH:(t + 1) * CH].rearrange("c p v -> p c v"),
                       vt, waits=(vtok,))
        fr[("vt",)] = ("st", stok)
        fr[("lnfm",)] = g.plan.tok("pe")
        p.barrier()


A2_STAGE = 4


def build_phase_A2(g, sb, x0, rA, qfmA, kfmA, vaugA, rqfmA):
    nc, p = g.nc, g.plan
    rkvw = sb("A2_rkvw", [128, KC, 512], BF16)
    routw = sb("A2_routw", [128, 2, 1024], BF16)
    outw = sb("A2_outw", [128, KC, 1024], BF16)
    rkvb = sb("A2_rkvb", [128, 4], F32)
    triu = sb("A2_triu", [128, 128], BF16)
    trils = sb("A2_trils", [128, 128], BF16)
    latmask = sb("A2_latmask", [64, 32, 128], BF16)
    zvec1 = sb("A2_zvec1", [128, 1], F32)
    zvec64 = sb("A2_zvec64", [128, 1], F32)
    rows = {}
    for nm, wd in (("vb", 1024), ("rvb", 256), ("routb", 1024), ("outb", 1024),
                   ("glw", 64), ("grw", 64)):
        rows[nm] = sb(f"A2_row_{nm}", [1, wd], F32)
    g.loadc(rkvw, g.H["A_rkvw"].rearrange("(c p) o -> p c o", p=128))
    g.loadc(routw, g.H["A_routw"].rearrange("(c p) o -> p c o", p=128))
    g.loadc(outw, g.H["A_outw"].rearrange("(c p) o -> p c o", p=128))
    g.load(rkvb, g.H["A_rkvb"])
    g.load(triu, g.H["c_triu"])
    g.load(trils, g.H["c_trils"])
    g.load(latmask, g.H["c_latmask"])
    g.load(zvec1, g.H["c_zvec1"])
    g.load(zvec64, g.H["c_zvec64"])
    g.load(rows["vb"], g.H["A_vb_row"])
    g.load(rows["rvb"], g.H["A_rvb_row"])
    g.load(rows["routb"], g.H["A_routb_row"])
    g.load(rows["outb"], g.H["A_outb_row"])
    g.load(rows["glw"], g.H["A_glw"])
    g.load(rows["grw"], g.H["A_grw"])
    wtok = g.gate_loads()
    vb_b = sb("A2_vb_b", [128, 1024], F32)
    rvb_b = sb("A2_rvb_b", [128, 256], F32)
    routb_b = sb("A2_routb_b", [128, 1024], F32)
    outb_b = sb("A2_outb_b", [128, 1024], F32)
    glb = sb("A2_glb", [128, 64], F32)
    grb = sb("A2_grb", [128, 64], F32)
    bt = {}
    for nm, dst in (("vb", vb_b), ("rvb", rvb_b), ("routb", routb_b),
                    ("outb", outb_b), ("glw", glb), ("grw", grb)):
        bt[nm] = g.bias_bcast(dst, rows[nm], dst.shape[-1], waits=(wtok,))
    bctok = g.plan.tok("dve")
    pooled = sb("A2_pooled", [64, 1024], BF16)
    pooled_fm = sb("A2_pooled_fm", [128, KC, 64], BF16)
    rksp = sb("A2_rksp", [64, 2, 2, 64], BF16)
    rvaug = sb("A2_rvaug", [64, 4, 65], BF16)
    t1 = _dv(g, lambda e: e.tensor_copy(pooled[:, 0:512], g.pb[6][0:64, :]))
    t2 = _dv(g, lambda e: e.tensor_copy(pooled[:, 512:1024], g.pb[7][0:64, :]))
    g.bankfree[6] = t1
    g.bankfree[7] = t2
    for k in range(KC):
        bank = k % 2
        allw = (t1, t2, g.bankfree[bank], g.consts["ident_tok"])
        p.add("pe",
              (lambda k=k, bank=bank: (lambda e: e.matmul(
                  g.pb[bank][:, 0:64], pooled[0:64, k * 128:(k + 1) * 128],
                  g.consts["ident"][0:64, 0:64], start=True, stop=True)))(),
              waits=allw, inc=("pe", 1))
        petok = g.plan.tok("pe")
        e1 = _dv(g, (lambda k=k, bank=bank: (
            lambda e: e.tensor_copy(pooled_fm[:, k, :], g.pb[bank][:, 0:64])))(),
                 waits=(petok,))
        g.bankfree[bank] = e1
    pftok = g.plan.tok("dve")
    rketok = None
    for oc in range(2):
        bank = 2 + oc
        pairs = [(rkvw[:, dc, oc * 128:(oc + 1) * 128], pooled_fm[:, dc, :])
                 for dc in range(KC)]
        petok = g.mm(bank, pairs, waits=[pftok, wtok])
        for par in range(2):
            rketok = p.add(
                "act",
                (lambda oc=oc, bank=bank, par=par: (lambda e: e.activation(
                    rksp[:, par, oc, :], g.pb[bank][par * 64:(par + 1) * 64, 0:64],
                    AF.Identity,
                    bias=rkvb[par * 64:(par + 1) * 64, oc:oc + 1])))(),
                waits=(petok,), inc=("act", 1))
        g.bankfree[bank] = ("act", rketok)
    rktok = ("act", rketok)
    pairs = [(pooled_fm[:, dc, :], rkvw[:, dc, 256:512]) for dc in range(KC)]
    petok = g.mm(4, pairs, waits=[pftok, wtok], out=g.pb[4][0:64, 0:256])
    mtok = _dv(g, lambda e: e.memset(rvaug[:, :, 64:65], 1.0))
    rvtok = _dv(g, lambda e: e.tensor_copy(
        rvaug[:, :, 0:64], g.pb[4][0:64, 0:256].rearrange("p (h v) -> p h v", v=64)),
                waits=(petok, mtok))
    g.bankfree[4] = rvtok
    if "dbg_rvaug" in g.H:
        g.store(g.H["dbg_rvaug"], rvaug, waits=(rvtok,))
        g.store(g.H["dbg_rksp"], rksp, waits=(rktok,))
        g.store(g.H["dbg_pooledfm"], pooled_fm, waits=(pftok,))
    qsp = sb("A2_qsp", [64, 2, KC, TT], BF16)
    ksp = sb("A2_ksp", [64, 2, KC, 640], BF16)
    vwin = sb("A2_vwin", [128, 5, 16 * 65], BF16)
    vwin4 = vwin.rearrange("p c (h v) -> p c h v", v=65)
    rqsp = sb("A2_rqsp", [64, 2, 2, TT], BF16)
    x0t = sb("A2_x0t", [128, CH, D], F32)
    Ea = sb("A2_Ea", [128, 16, 128], BF16)
    Eb = sb("A2_Eb", [128, 16, 128], BF16)
    El = sb("A2_El", [64, 4, 128], BF16)
    Uall = sb("A2_Uall", [128, 16, 65], F32)
    Ul = sb("A2_Ul", [128, 4, 65], F32)
    zc = sb("A2_zc", [128, 16], F32)
    rzv = sb("A2_rzv", [128, 16], F32)
    zcl = sb("A2_zcl", [128, 4], F32)
    rzl = sb("A2_rzl", [128, 4], F32)
    local = sb("A2_local", [128, 16, 64], F32)
    latout = sb("A2_latout", [128, 256], BF16)
    latfm = sb("A2_latfm", [128, 2, 128], BF16)
    remote = sb("A2_remote", [128, 1024], F32)
    gwork = sb("A2_gwork", [128, 16, 64], F32)
    gpre = sb("A2_gpre", [128, 16], F32)
    gpre2 = sb("A2_gpre2", [128, 16], F32)
    gv = sb("A2_gv", [128, 16], F32)
    d1 = sb("A2_d1", [128, 16, 64], F32)
    mixed = sb("A2_mixed", [128, 1024], BF16)
    mixfm = sb("A2_mixfm", [128, KC, 128], BF16)
    rpb = sb("A2_rpb", [128, 1024], F32)
    rAout = sb("A2_rAout", [128, CH, D], F32)
    fr = {}
    for t in range(NT):
        for par in range(2):
            qtok = ("ld", g.load(qsp[:, par], qfmA[:, par * 64:(par + 1) * 64,
                                                   t * TT:(t + 1) * TT]
                                 .rearrange("c p t -> p c t"),
                                 waits=(fr.get(("qfm",), ("ph", 0)), wtok)))
            if t == 0:
                ktok = ("ld", g.load(ksp[:, par, :, 128:640],
                                     kfmA[:, par * 64:(par + 1) * 64, 0:TT]
                                     .rearrange("c p t -> p c t"),
                                     waits=(fr.get(("kwin",), ("ph", 0)), wtok)))
            else:
                ktok = ("ld", g.load(ksp[:, par],
                                     kfmA[:, par * 64:(par + 1) * 64,
                                          t * TT - 128:(t + 1) * TT]
                                     .rearrange("c p t -> p c t"),
                                     waits=(fr.get(("kwin",), ("ph", 0)), wtok)))
            rqtok = ("ld", g.load(rqsp[:, par],
                                  rqfmA[:, par * 64:(par + 1) * 64, t * TT:(t + 1) * TT]
                                  .rearrange("c p t -> p c t"),
                                  waits=(fr.get(("rqt",), ("ph", 0)), wtok)))
        if t == 0:
            vltok = ("ld", g.load(vwin[:, 1:5, :],
                                  vaugA[0:CH].rearrange("c p v -> p c v"),
                                  waits=(fr.get(("vwin",), ("ph", 0)), wtok)))
        else:
            vltok = ("ld", g.load(vwin, vaugA[t * CH - 1:(t + 1) * CH]
                                  .rearrange("c p v -> p c v"),
                                  waits=(fr.get(("vwin",), ("ph", 0)), wtok)))
        xtok = ("ld", g.load(x0t, x0.rearrange("(n c p) d -> n p c d", p=128, c=CH)[t],
                             waits=(fr.get(("x0t",), ("ph", 0)), wtok)))
        outtok = None
        for qc in range(CH):
            gc = t * CH + qc
            if A2_STAGE < 1:
                outtok = _dv(g, (lambda qc=qc: (lambda e: e.tensor_copy(
                    rAout[:, qc, :], x0t[:, qc, :])))(),
                             waits=(xtok, fr.get(("rAout",), ("ph", 0))))
                continue
            halves = [("b", Eb, trils, qc + 1)]
            if gc > 0:
                halves.insert(0, ("a", Ea, triu, qc))
            for hname, Ebuf, maskt, vslot in halves:
                ka = qc * 128 if hname == "a" else qc * 128 + 128
                for hg in range(4):
                    bank = hg % 2
                    exw = [ktok, qtok, fr.get(("Euse",), ("ph", 0))]
                    for j in range(4):
                        h = hg * 4 + j
                        lh = ksp[:, h % 2, h // 2, ka:ka + 128]
                        rh = qsp[:, h % 2, h // 2, qc * 128:(qc + 1) * 128]
                        p.add("pe",
                              (lambda lh=lh, rh=rh, bank=bank, j=j: (
                                  lambda e: e.matmul(
                                      g.pb[bank][:, j * 128:(j + 1) * 128], lh, rh,
                                      start=True, stop=True)))(),
                              waits=tuple(exw) + (g.bankfree[bank],) if j == 0 else (),
                              inc=("pe", 1) if j == 3 else None)
                    petok = g.plan.tok("pe")
                    aetok = _ac(g, (lambda hg=hg, bank=bank, Ebuf=Ebuf: (
                        lambda e: e.activation(
                            Ebuf[:, hg * 4:(hg + 1) * 4, :].rearrange("p h q -> p (h q)"),
                            g.pb[bank][:, :512], AF.Exp)))(),
                                waits=(petok,))
                    g.bankfree[bank] = aetok
                    fr[("E", hname, hg)] = aetok
                msktok = _dv(g, (lambda Ebuf=Ebuf, maskt=maskt: (
                    lambda e: e.tensor_tensor(
                        Ebuf, Ebuf,
                        maskt.unsqueeze(1).broadcast_to([128, 16, 128]), ALU.mult)))(),
                             waits=(fr[("E", hname, 0)], fr[("E", hname, 1)],
                                    fr[("E", hname, 2)], fr[("E", hname, 3)], wtok))
                fr[("Emask", hname)] = msktok
            ubanks = {2: list(range(0, 7)), 3: list(range(7, 14)), 4: [14, 15]}
            for bank, hs in ubanks.items():
                for idx, h in enumerate(hs):
                    col = idx * 66
                    w0 = (fr[("Emask", "b")],) if gc == 0 else (
                        fr[("Emask", "a")], fr[("Emask", "b")])
                    w0 = w0 + (vltok, g.bankfree[bank])
                    mm_list = []
                    if gc > 0:
                        mm_list.append((Ea[:, h, :], vwin4[:, qc, h, :]))
                    mm_list.append((Eb[:, h, :], vwin4[:, qc + 1, h, :]))
                    for i, (lhh, rhh) in enumerate(mm_list):
                        p.add("pe",
                              (lambda lhh=lhh, rhh=rhh, bank=bank, col=col, i=i,
                                      n=len(mm_list): (
                                  lambda e: e.matmul(
                                      g.pb[bank][:, col:col + 65], lhh, rhh,
                                      start=(i == 0), stop=(i == n - 1))))(),
                              waits=w0 if (i == 0 and idx == 0) else (),
                              inc=("pe", 1) if (i == len(mm_list) - 1 and idx == len(hs) - 1) else None)
                petok = g.plan.tok("pe")
                etok = _dv(g, (lambda bank=bank, hs=hs: (
                    lambda e: e.tensor_copy(
                        Uall[:, hs[0]:hs[-1] + 1, :],
                        g.pb[bank][:, 0:len(hs) * 66]
                        .rearrange("p (h v) -> p h v", v=66)[:, :, 0:65])))(),
                           waits=(petok,))
                g.bankfree[bank] = etok
            fr[("Euse",)] = g.plan.tok("pe")
            ucp = g.plan.tok("dve")
            zt = _dv(g, lambda e: e.tensor_scalar(zc, Uall[:, :, 64], 1e-30, None, ALU.max),
                     waits=(ucp,))
            rt = _dv(g, lambda e: e.reciprocal(rzv, zc), waits=(zt,))
            l1 = _dv(g, lambda e: e.tensor_tensor(
                local, Uall[:, :, 0:64], rzv.unsqueeze(2).broadcast_to([128, 16, 64]),
                ALU.mult), waits=(rt,))
            l2 = _dv(g, lambda e: e.tensor_tensor(
                local.rearrange("p h v -> p (h v)"), local.rearrange("p h v -> p (h v)"),
                vb_b, ALU.add), waits=(l1, bctok))
            if gc == 0:
                l2 = _dv(g, lambda e: e.tensor_scalar(
                    local.rearrange("p h v -> p (h v)"),
                    local.rearrange("p h v -> p (h v)"), zvec1[:, 0:1], None, ALU.mult),
                         waits=(l2, wtok))
            if A2_STAGE < 2:
                outtok = _dv(g, (lambda qc=qc: (lambda e: e.tensor_copy(
                    rAout[:, qc, :], local.rearrange("p h v -> p (h v)"))))(),
                             waits=(l2, fr.get(("rAout",), ("ph", 0))))
                continue
            bank = 5
            for lh in range(4):
                lhsl = rksp[:, lh % 2, lh // 2, :]
                rhsl = rqsp[:, lh % 2, lh // 2, qc * 128:(qc + 1) * 128]
                p.add("pe",
                      (lambda lhsl=lhsl, rhsl=rhsl, lh=lh, bank=bank: (
                          lambda e: e.matmul(g.pb[bank][0:64, lh * 128:(lh + 1) * 128],
                                             lhsl, rhsl, start=True, stop=True)))(),
                      waits=(rktok, rqtok, g.bankfree[bank]) if lh == 0 else (),
                      inc=("pe", 1) if lh == 3 else None)
            petok = g.plan.tok("pe")
            letok = _ac(g, (lambda bank=bank: (lambda e: e.activation(
                El.rearrange("p h q -> p (h q)"), g.pb[bank][0:64, :], AF.Exp)))(),
                        waits=(petok,))
            g.bankfree[bank] = letok
            lmtok = _dv(g, (lambda gc=gc: (lambda e: e.tensor_tensor(
                El, El, latmask[:, gc, :].unsqueeze(1).broadcast_to([64, 4, 128]),
                ALU.mult)))(), waits=(letok, wtok))
            if "dbg_El" in g.H and gc == 2:
                g.store(g.H["dbg_El"], El, waits=(lmtok,))
            for lh in range(4):
                p.add("pe",
                      (lambda lh=lh: (lambda e: e.matmul(
                          g.pb[5][:, lh * 66:lh * 66 + 65], El[:, lh, :],
                          rvaug[:, lh, :], start=True, stop=True)))(),
                      waits=(lmtok, rvtok, g.bankfree[5]) if lh == 0 else (),
                      inc=("pe", 1) if lh == 3 else None)
            petok = g.plan.tok("pe")
            uletok = _dv(g, lambda e: e.tensor_copy(
                Ul, g.pb[5][:, 0:264].rearrange("p (h v) -> p h v", v=66)[:, :, 0:65]),
                         waits=(petok,))
            g.bankfree[5] = uletok
            if A2_STAGE == 13:
                outtok = _dv(g, (lambda qc=qc: (lambda e: e.tensor_copy(
                    rAout[:, qc, 0:260], Ul.rearrange("p h v -> p (h v)"))))(),
                             waits=(uletok, fr.get(("rAout",), ("ph", 0))))
                continue
            z2 = _dv(g, lambda e: e.tensor_scalar(zcl, Ul[:, :, 64], 1e-30, None, ALU.max),
                     waits=(uletok,))
            r2 = _dv(g, lambda e: e.reciprocal(rzl, zcl), waits=(z2,))
            lo1 = _dv(g, lambda e: e.tensor_tensor(
                latout.rearrange("p (h v) -> p h v", v=64), Ul[:, :, 0:64],
                rzl.unsqueeze(2).broadcast_to([128, 4, 64]), ALU.mult),
                      waits=(r2, fr.get(("latout",), ("ph", 0))))
            lo2 = _dv(g, lambda e: e.tensor_tensor(latout, latout, rvb_b, ALU.add),
                      waits=(lo1, bctok))
            if gc == 0:
                lo2 = _dv(g, lambda e: e.tensor_scalar(latout, latout, zvec64[:, 0:1],
                                                       None, ALU.mult),
                          waits=(lo2, wtok))
            if A2_STAGE == 12:
                outtok = _dv(g, (lambda qc=qc: (lambda e: e.tensor_copy(
                    rAout[:, qc, 0:256], latout)))(),
                             waits=(lo2, fr.get(("rAout",), ("ph", 0))))
                continue
            tfm = g.transpose4(0, [latout[:, 0:128], latout[:, 128:256]],
                               [latfm[:, 0, :], latfm[:, 1, :]],
                               waits=(lo2, fr.get(("latfm",), ("ph", 0))),
                               evac_eng="dve")
            fr[("latout",)] = g.plan.tok("pe")
            remtok = None
            for half in range(2):
                bank = 2 + half
                pairs = [(latfm[:, j, :], routw[:, j, half * 512:(half + 1) * 512])
                         for j in range(2)]
                petok = g.mm(bank, pairs, waits=[tfm, wtok])
                remtok = _dv(g, (lambda half=half, bank=bank: (
                    lambda e: e.tensor_tensor(
                        remote[:, half * 512:(half + 1) * 512], g.pb[bank][:, :512],
                        routb_b[:, half * 512:(half + 1) * 512], ALU.add)))(),
                             waits=(petok, bctok))
                g.bankfree[bank] = remtok
            fr[("latfm",)] = g.plan.tok("pe")
            if A2_STAGE < 3:
                outtok = _dv(g, (lambda qc=qc: (lambda e: e.tensor_copy(
                    rAout[:, qc, :], remote)))(),
                             waits=(remtok, fr.get(("rAout",), ("ph", 0))))
                continue
            g1 = _dv(g, lambda e: e.tensor_tensor(
                gwork, local, glb.unsqueeze(1).broadcast_to([128, 16, 64]), ALU.mult),
                     waits=(l2, bctok))
            g2 = _dv(g, lambda e: e.reduce_sum(gpre, gwork, axis=AX.X), waits=(g1,))
            g3 = _dv(g, lambda e: e.tensor_tensor(
                gwork, remote.rearrange("p (h v) -> p h v", v=64),
                grb.unsqueeze(1).broadcast_to([128, 16, 64]), ALU.mult),
                     waits=(remtok, g2))
            g4 = _dv(g, lambda e: e.reduce_sum(gpre2, gwork, axis=AX.X), waits=(g3,))
            g5 = _dv(g, lambda e: e.tensor_tensor(gpre, gpre, gpre2, ALU.add),
                     waits=(g4,))
            g6 = _ac(g, lambda e: e.activation(gv, gpre, AF.Sigmoid), waits=(g5,))
            m1 = _dv(g, lambda e: e.tensor_tensor(
                d1, local, remote.rearrange("p (h v) -> p h v", v=64), ALU.subtract),
                     waits=(l2, remtok))
            m2 = _dv(g, lambda e: e.tensor_tensor(
                d1, d1, gv.unsqueeze(2).broadcast_to([128, 16, 64]), ALU.mult),
                     waits=(m1, g6))
            m3 = _dv(g, lambda e: e.tensor_tensor(
                mixed.rearrange("p (h v) -> p h v", v=64), d1,
                remote.rearrange("p (h v) -> p h v", v=64), ALU.add),
                     waits=(m2, fr.get(("mixed",), ("ph", 0))))
            if A2_STAGE < 4:
                outtok = _dv(g, (lambda qc=qc: (lambda e: e.tensor_copy(
                    rAout[:, qc, :], mixed)))(),
                             waits=(m3, fr.get(("rAout",), ("ph", 0))))
                continue
            mfw = fr.get(("mixfm",), ("ph", 0))
            tfm1 = g.transpose4(0, [mixed[:, j * 128:(j + 1) * 128] for j in range(4)],
                                [mixfm[:, j, :] for j in range(4)],
                                waits=(m3, mfw), evac_eng="dve")
            tfm2 = g.transpose4(1, [mixed[:, (4 + j) * 128:(5 + j) * 128] for j in range(4)],
                                [mixfm[:, 4 + j, :] for j in range(4)],
                                waits=(m3, mfw), evac_eng="dve")
            fr[("mixed",)] = g.plan.tok("pe")
            rp = _dv(g, (lambda qc=qc: (lambda e: e.tensor_tensor(
                rpb, x0t[:, qc, :], outb_b, ALU.add)))(), waits=(xtok, bctok))
            for half in range(2):
                bank = 2 + half
                pairs = [(mixfm[:, kc, :], outw[:, kc, half * 512:(half + 1) * 512])
                         for kc in range(KC)]
                petok = g.mm(bank, pairs, waits=[tfm1, tfm2, wtok])
                outtok = _dv(g, (lambda qc=qc, half=half, bank=bank: (
                    lambda e: e.tensor_tensor(
                        rAout[:, qc, half * 512:(half + 1) * 512], g.pb[bank][:, :512],
                        rpb[:, half * 512:(half + 1) * 512], ALU.add)))(),
                             waits=(petok, rp, fr.get(("rAout",), ("ph", 0))))
                g.bankfree[bank] = outtok
            fr[("mixfm",)] = g.plan.tok("pe")
        stok = g.store(rA.rearrange("(n c p) d -> n p c d", p=128, c=CH)[t], rAout,
                       waits=(outtok,))
        fr[("rAout",)] = ("st", stok)
        fr[("qfm",)] = g.plan.tok("pe")
        fr[("kwin",)] = g.plan.tok("pe")
        fr[("vwin",)] = g.plan.tok("pe")
        fr[("rqt",)] = g.plan.tok("pe")
        fr[("x0t",)] = g.plan.tok("dve")
        p.barrier()



def build_phase_B(g, rA_name="rA", rB_name="rB"):
    nc, p = g.nc, g.plan
    rA = g.H[rA_name]
    rB = g.H[rB_name]
    ln1fm = g.dram("ln1fm", [KC, 128, S], BF16, "Internal")
    from contextlib import ExitStack
    with ExitStack() as st:
        sb = lambda n, shp, dt: st.enter_context(nc.sbuf_tensor(n, list(shp), dt)).ap()
        we_sb = sb("B_we", [128, 32, 8], F32)
        with ExitStack() as st1:
            sb1 = lambda n, shp, dt: st1.enter_context(nc.sbuf_tensor(n, list(shp), dt)).ap()
            build_phase_B1(g, sb1, rA, rB, ln1fm, we_sb)
        p.barrier()
        with ExitStack() as st2:
            sb2 = lambda n, shp, dt: st2.enter_context(nc.sbuf_tensor(n, list(shp), dt)).ap()
            build_phase_B2(g, sb2, rB, ln1fm, we_sb)
    p.barrier()


def build_phase_B1(g, sb, rA, rB, ln1fm, we_sb):
    nc, p = g.nc, g.plan
    gwT = sb("B_gwT_sb", [128, KC, 8], BF16)
    identf = sb("B_identf", [128, 128], F32)
    b2sb = sb("B_b2_sb", [8, 1024], F32)
    g.loadc(gwT, g.H["B_gwT"].rearrange("(c p) o -> p c o", p=128))
    g.load(identf, g.H["c_identf"])
    g.load(b2sb, g.H["B_b2"])
    wtok = g.gate_loads()
    bufs = {
        "sbx": [sb("B_sbx0", [128, CH, D], F32), sb("B_sbx1", [128, CH, D], F32)],
        "sq": sb("B_sq", [128, CH * D], F32),
        "sums": sb("B_sums", [128, CH], F32), "sumsq": sb("B_sumsq", [128, CH], F32),
        "mean": sb("B_mean", [128, CH], F32), "var": sb("B_var", [128, CH], F32),
        "msq": sb("B_msq", [128, CH], F32), "m2": sb("B_m2", [128, CH], F32),
        "std": sb("B_std", [128, CH], F32), "rstd": sb("B_rstd", [128, CH], F32),
        "lntm": sb("B_lntm", [128, CH, D], BF16),
        "free": {},
    }
    lnfm = sb("B_lnfm", [128, KC, TT], BF16)
    lg = sb("B_lg", [128, 8], F32)
    m1 = sb("B_m1", [128, 1], F32)
    nm1 = sb("B_nm1", [128, 1], F32)
    msk1 = sb("B_msk1", [128, 8], F32)
    lgm = sb("B_lgm", [128, 8], F32)
    m2t = sb("B_m2t", [128, 1], F32)
    msk2 = sb("B_msk2", [128, 8], F32)
    e8 = sb("B_e8", [128, 8], F32)
    s8 = sb("B_s8", [128, 1], F32)
    rs = sb("B_rs", [128, 1], F32)
    wef = sb("B_wef", [128, 8], F32)
    weT = sb("B_weT", [8, 128], F32)
    rBo = sb("B_rBo", [128, CH, D], F32)
    fr = bufs["free"]
    for t in range(NT):
        xtok, lntok = emit_ln_tiles(g, "B", rA, t, bufs, waits_extra=(wtok,))
        fmtok = emit_tp_to_fm(g, bufs["lntm"], lnfm, lntok, banks=(0, 1),
                              fr=fr, key=("lnfm",))
        fr[("lntm",)] = g.plan.tok("pe")
        stok = g.store(ln1fm[:, :, t * TT:(t + 1) * TT].rearrange("c p t -> p c t"),
                       lnfm, waits=(fmtok,))
        outtok = None
        for qc in range(CH):
            gidx = t * CH + qc
            bank = 2 + (qc % 2)
            pairs = [(lnfm[:, dc, qc * 128:(qc + 1) * 128], gwT[:, dc, :])
                     for dc in range(KC)]
            petok = g.mm(bank, pairs, waits=[fmtok, wtok])
            c0 = _dv(g, (lambda bank=bank: (lambda e: e.tensor_copy(lg, g.pb[bank][:, 0:8])))(),
                     waits=(petok,))
            g.bankfree[bank] = c0
            c1 = _dv(g, lambda e: e.reduce_max(m1, lg, axis=AX.X), waits=(c0,))
            c2 = _dv(g, lambda e: e.reduce_max(nm1, lg, axis=AX.X, negate=True), waits=(c0,))
            c3 = _dv(g, lambda e: e.tensor_scalar(msk1, lg, m1[:, 0:1], None, ALU.is_ge),
                     waits=(c1,))
            c4 = _dv(g, lambda e: e.scalar_tensor_tensor(lgm, msk1, -1e9, lg, ALU.mult, ALU.add),
                     waits=(c3,))
            c5 = _dv(g, lambda e: e.reduce_max(m2t, lgm, axis=AX.X), waits=(c4,))
            c6 = _dv(g, lambda e: e.tensor_scalar(msk2, lgm, m2t[:, 0:1], None, ALU.is_ge),
                     waits=(c5,))
            a1 = _ac(g, lambda e: e.activation(e8, lg, AF.Exp, bias=nm1[:, 0:1],
                                               accum_out=s8[:, 0:1]),
                     waits=(c2,))
            c7 = _dv(g, lambda e: e.reciprocal(rs, s8), waits=(a1,))
            c8 = _dv(g, lambda e: e.tensor_tensor(msk1, msk1, msk2, ALU.add), waits=(c6,))
            c9 = _dv(g, lambda e: e.tensor_tensor(wef, e8, msk1, ALU.mult), waits=(c8, a1))
            c10 = _dv(g, (lambda gidx=gidx: (lambda e: e.tensor_scalar(
                we_sb[:, gidx, :], wef, rs[:, 0:1], None, ALU.mult)))(),
                      waits=(c9, c7))
            bank2 = 4 + (qc % 2)
            p.add("pe",
                  (lambda gidx=gidx, bank2=bank2: (lambda e: e.matmul(
                      g.pb[bank2][0:8, :128], we_sb[:, gidx, :], identf,
                      start=True, stop=True)))(),
                  waits=(c10, wtok, g.bankfree[bank2]), inc=("pe", 1))
            ptk = g.plan.tok("pe")
            cw = _dv(g, (lambda bank2=bank2: (lambda e: e.tensor_copy(
                weT, g.pb[bank2][0:8, :128])))(), waits=(ptk,))
            g.bankfree[bank2] = cw
            for half in range(2):
                bank3 = 6 + half
                p.add("pe",
                      (lambda half=half, bank3=bank3: (lambda e: e.matmul(
                          g.pb[bank3][:, :512], weT,
                          b2sb[:, half * 512:(half + 1) * 512],
                          start=True, stop=True)))(),
                      waits=(cw, wtok, g.bankfree[bank3]), inc=("pe", 1))
                ptk2 = g.plan.tok("pe")
                outtok = _dv(g, (lambda qc=qc, half=half, bank3=bank3, t=t: (
                    lambda e: e.tensor_tensor(
                        rBo[:, qc, half * 512:(half + 1) * 512],
                        g.pb[bank3][:, :512],
                        bufs["sbx"][t % 2][:, qc, half * 512:(half + 1) * 512],
                        ALU.add)))(),
                             waits=(ptk2, xtok, fr.get(("rBo",), ("ph", 0))))
                g.bankfree[bank3] = outtok
        sttok = g.store(rB.rearrange("(n c p) d -> n p c d", p=128, c=CH)[t], rBo,
                        waits=(outtok,))
        fr[("rBo",)] = ("st", sttok)
        fr[("lnfm",)] = g.plan.tok("pe")
        fr[("sbx", t % 2)] = ("dve", g.plan.cnt["dve"])
        p.barrier()
        if "dbg_we" in g.H:
            g.store(g.H["dbg_we"], we_sb, waits=(("dve", g.plan.cnt["dve"]),))


B2_EXPERTS = 8


def build_phase_B2(g, sb, rB, ln1fm, we_sb):
    nc, p = g.nc, g.plan
    w1e = sb("B2_w1e", [128, KC, 4096], BF16)
    w2e = sb("B2_w2e", [128, 32, 1024], BF16)
    b1e = sb("B2_b1e", [128, 32], F32)
    lnt = sb("B2_lnt", [128, KC, TT], BF16)
    hfull = sb("B2_h", [128, 32, TT], BF16)
    ytmp = sb("B2_ytmp", [128, CH, D], F32)
    fr = {}
    for e in range(B2_EXPERTS):
        w1tok = ("ld", g.loadc(w1e, g.H["B_w1"][:, e, :].rearrange("(c p) f -> p c f", p=128),
                               waits=(fr.get(("w1",), ("ph", 0)),)))
        w2tok = ("ld", g.loadc(w2e, g.H["B_w2"][:, e, :].rearrange("(c p) o -> p c o", p=128),
                               waits=(fr.get(("w2",), ("ph", 0)),)))
        b1tok = ("ld", g.load(b1e, g.H["B_b1"][e], waits=(fr.get(("b1",), ("ph", 0)),)))
        for t in range(NT):
            ltok = ("ld", g.load(lnt, ln1fm[:, :, t * TT:(t + 1) * TT]
                                 .rearrange("c p t -> p c t"),
                                 waits=(fr.get(("lnt",), ("ph", 0)), b1tok)))
            hetok = None
            for fs in range(32):
                bank = fs % 2
                pairs = [(w1e[:, dc, fs * 128:(fs + 1) * 128], lnt[:, dc, :])
                         for dc in range(KC)]
                petok = g.mm(bank, pairs, waits=[w1tok, b1tok, ltok,
                                                 fr.get(("h",), ("ph", 0))])
                hetok = p.add(
                    "act",
                    (lambda fs=fs, bank=bank: (lambda e2: e2.activation(
                        hfull[:, fs, :], g.pb[bank][:, :TT], AF.Gelu_apprx_tanh,
                        bias=b1e[:, fs:fs + 1])))(),
                    waits=(petok,), inc=("act", 1))
                g.bankfree[bank] = ("act", hetok)
            htok = ("act", hetok)
            ytok = None
            for qc in range(CH):
                for half in range(2):
                    bank = 2 + ((qc * 2 + half) % 4)
                    pairs = [(hfull[:, fs, qc * 128:(qc + 1) * 128],
                              w2e[:, fs, half * 512:(half + 1) * 512])
                             for fs in range(32)]
                    petok = g.mm(bank, pairs, waits=[htok, w2tok,
                                                     fr.get(("ytmp",), ("ph", 0))])
                    ytok = _dv(g, (lambda qc=qc, half=half, bank=bank, t=t, e=e: (
                        lambda e2: e2.tensor_scalar(
                            ytmp[:, qc, half * 512:(half + 1) * 512],
                            g.pb[bank][:, :512],
                            we_sb[:, t * CH + qc, e:e + 1], None, ALU.mult)))(),
                               waits=(petok,))
                    g.bankfree[bank] = ytok
            fr[("h",)] = g.plan.tok("pe")
            atok = g.accum(rB.rearrange("(n c p) d -> n p c d", p=128, c=CH)[t],
                           ytmp, sem="acc", waits=(ytok, fr.get(("acc",), ("ph", 0))))
            fr[("ytmp",)] = ("acc", atok)
            fr[("acc",)] = ("acc", atok)
            fr[("lnt",)] = g.plan.tok("pe")
            p.barrier()
        fr[("w1",)] = g.plan.tok("pe")
        fr[("w2",)] = g.plan.tok("pe")
        fr[("b1",)] = g.plan.tok("act")



def build_phase_D(g, rC_name="rC", y_name="y"):
    nc, p = g.nc, g.plan
    rC = g.H[rC_name]
    yout = g.H[y_name]
    from contextlib import ExitStack
    with ExitStack() as st:
        sb = lambda n, shp, dt: st.enter_context(nc.sbuf_tensor(n, list(shp), dt)).ap()
        build_phase_D_inner(g, sb, rC, yout)
    p.barrier()


D_STAGE = 9
D_SAFE = 2


def build_phase_D_inner(g, sb, rC, yout):
    nc, p = g.nc, g.plan
    pww = sb("D_pww_sb", [128, KC, 1024], BF16)
    owT = sb("D_owT_sb", [128, KC, 1024], BF16)
    ffdT = sb("D_ffdT_sb", [128, KC, 256], BF16)
    ffuT = sb("D_ffuT_sb", [128, 2, 1024], BF16)
    rwT = sb("D_rwT_sb", [128, KC, 8], BF16)
    ewe = sb("D_ewe_sb", [128, KC, 1024], BF16)
    ebrow = sb("D_ebrow_sb", [1, 8192], BF16)
    dwk = sb("D_dwk_sb", [128, KC, 3], F32)
    pwb = sb("D_pwb_sb", [128, KC], F32)
    dbf = sb("D_dbf_sb", [128, KC], F32)
    ffdb = sb("D_ffdb_sb", [128, 2], F32)
    onesbf = sb("D_onesbf", [1, 128], BF16)
    g.loadc(pww, g.H["D_pww"].rearrange("(c p) o -> p c o", p=128))
    g.loadc(owT, g.H["D_owT"].rearrange("(c p) o -> p c o", p=128))
    g.loadc(ffdT, g.H["D_ffdT"].rearrange("(c p) o -> p c o", p=128))
    g.loadc(ffuT, g.H["D_ffuT"].rearrange("(c p) o -> p c o", p=128))
    g.loadc(rwT, g.H["D_rwT"].rearrange("(c p) o -> p c o", p=128))
    g.loadc(ebrow, g.H["D_ebrow"])
    g.loadc(onesbf, g.H["c_ones1"])
    g.load(dwk, g.H["D_dwk"].rearrange("(c p) k -> p c k", p=128))
    g.load(pwb, g.H["D_pwbf"])
    g.load(dbf, g.H["D_dbf"])
    g.load(ffdb, g.H["D_ffdbf"])
    rows = {}
    for nm, wd in (("rb", 8), ("ob", 1024), ("nw", 1024), ("ffub", 1024)):
        rows[nm] = sb(f"D_row_{nm}", [1, wd], F32)
    g.load(rows["rb"], g.H["D_rb"])
    g.load(rows["ob"], g.H["D_ob"])
    g.load(rows["nw"], g.H["D_nw"])
    g.load(rows["ffub"], g.H["D_ffub"])
    wtok = g.gate_loads()
    rbb = sb("D_rbb", [128, 8], F32)
    obb = sb("D_obb", [128, 1024], F32)
    nwb = sb("D_nwb", [128, 1024], F32)
    ffubb = sb("D_ffubb", [128, 1024], F32)
    for nm, dst in (("rb", rbb), ("ob", obb), ("nw", nwb), ("ffub", ffubb)):
        g.bias_bcast(dst, rows[nm], dst.shape[-1], waits=(wtok,))
    bctok = g.plan.tok("dve")

    _sbx = sb("D_sbx0", [128, CH, D], F32)
    bufs = {
        "sbx": [_sbx, _sbx],
        "sq": None,
        "sums": sb("D_sums", [128, CH], F32), "sumsq": sb("D_sumsq", [128, CH], F32),
        "mean": sb("D_mean", [128, CH], F32), "var": sb("D_var", [128, CH], F32),
        "msq": sb("D_msq", [128, CH], F32), "m2": sb("D_m2", [128, CH], F32),
        "std": sb("D_std", [128, CH], F32), "rstd": sb("D_rstd", [128, CH], F32),
        "lntm": sb("D_lntm", [128, CH, D], BF16),
        "free": {},
    }
    lnfm3 = sb("D_lnfm3", [128, KC, 514], BF16)
    tailsv = sb("D_tailsv", [128, KC, 2], BF16)
    tmpc = sb("D_tmpc", [128, 512], F32)
    xcbf = sb("D_xcbf", [128, KC, 512], BF16)
    xc2 = sb("D_xc2", [128, KC, 512], BF16)
    x1tm = sb("D_x1tm", [128, CH, D], BF16)
    lg = sb("D_lg", [128, 8], F32)
    m1 = sb("D_m1", [128, 1], F32)
    nm1 = sb("D_nm1", [128, 1], F32)
    msk1 = sb("D_msk1", [128, 8], F32)
    lgm = sb("D_lgm", [128, 8], F32)
    m2t = sb("D_m2t", [128, 1], F32)
    msk2 = sb("D_msk2", [128, 8], F32)
    e8 = sb("D_e8", [128, 8], F32)
    em8 = sb("D_em8", [128, 8], F32)
    s8 = sb("D_s8", [128, 1], F32)
    rs = sb("D_rs", [128, 1], F32)
    rw = sb("D_rw", [128, CH, 8], F32)
    comb = sb("D_comb", [128, CH, D], F32)
    combbf = sb("D_combbf", [128, D], BF16)
    sil = [sb("D_sil0", [128, 512], BF16), sb("D_sil1", [128, 512], BF16)]
    hbf = sb("D_hbf", [128, CH, D], BF16)
    hfm = sb("D_hfm", [128, KC, TT], BF16)
    ff1 = sb("D_ff1", [128, 2, TT], BF16)
    rrms4 = sb("D_rrms4", [128, CH], F32)
    ss1 = sb("D_ss1", [128, 1], F32)
    ss2 = sb("D_ss2", [128, 1], F32)
    rmsv = sb("D_rmsv", [128, 1], F32)
    rpb = sb("D_rpb", [128, D], F32)
    hf32 = sb("D_hf32", [128, D], F32)
    fr = bufs["free"]

    for t in range(NT):
        xtok, lntok = emit_ln_tiles(g, "D", rC, t, bufs, waits_extra=(wtok,))
        if t == 0:
            ttok = _dv(g, lambda e: e.memset(lnfm3[:, :, 0:2], 0.0),
                       waits=(fr.get(("lnfm",), ("ph", 0)),))
        else:
            ttok = _dv(g, lambda e: e.tensor_copy(lnfm3[:, :, 0:2], tailsv),
                       waits=(fr.get(("lnfm",), ("ph", 0)), fr.get(("tail",), ("ph", 0))))
        fmtok = emit_tp_to_fm(g, bufs["lntm"], lnfm3, lntok, banks=(0, 1),
                              fr=fr, key=("lnfm",), col_off=2)
        fr[("lntm",)] = g.plan.tok("pe")
        cvtok = None
        for dc in range(KC):
            c1 = _dv(g, (lambda dc=dc: (lambda e: e.tensor_scalar(
                tmpc, lnfm3[:, dc, 0:512], dwk[:, dc, 0:1], dbf[:, dc:dc + 1],
                ALU.mult, ALU.add)))(),
                     waits=(fmtok, ttok, wtok))
            c2 = _dv(g, (lambda dc=dc: (lambda e: e.scalar_tensor_tensor(
                tmpc, lnfm3[:, dc, 1:513], dwk[:, dc, 1:2], tmpc,
                ALU.mult, ALU.add)))(), waits=(c1,))
            cvtok = _dv(g, (lambda dc=dc: (lambda e: e.scalar_tensor_tensor(
                xcbf[:, dc, :], lnfm3[:, dc, 2:514], dwk[:, dc, 2:3], tmpc,
                ALU.mult, ALU.add)))(), waits=(c2, fr.get(("xcbf",), ("ph", 0))))
        pwtok = None
        for oc in range(KC):
            bank = 2 + (oc % 2)
            pairs = [(pww[:, dc, oc * 128:(oc + 1) * 128], xcbf[:, dc, :])
                     for dc in range(KC)]
            petok = g.mm(bank, pairs, waits=[cvtok, wtok,
                                             fr.get(("xc2",), ("ph", 0))])
            pwtok = p.add(
                "act",
                (lambda oc=oc, bank=bank: (lambda e: e.activation(
                    xc2[:, oc, :], g.pb[bank][:, :512], AF.Silu,
                    bias=pwb[:, oc:oc + 1])))(),
                waits=(petok,), inc=("act", 1))
            g.bankfree[bank] = ("act", pwtok)
        svtok = _dv(g, lambda e: e.tensor_copy(tailsv, lnfm3[:, :, 512:514]),
                    waits=(fmtok,))
        fr[("tail",)] = svtok
        x1tok = _dv(g, lambda e: e.tensor_tensor(
            lnfm3[:, :, 2:514], lnfm3[:, :, 2:514], xc2, ALU.add),
                    waits=(svtok, ("act", pwtok)))
        x1fm = lnfm3[:, :, 2:514]
        x1tmtok = None
        for qc in range(CH):
            for kg in range(2):
                srcs = [x1fm[:, kg * 4 + j, qc * 128:(qc + 1) * 128] for j in range(4)]
                dsts = [x1tm[:, qc, (kg * 4 + j) * 128:(kg * 4 + j + 1) * 128]
                        for j in range(4)]
                x1tmtok = g.transpose4(kg, srcs, dsts,
                                       waits=(x1tok, fr.get(("x1tm",), ("ph", 0))),
                                       evac_eng="dve")
        if D_STAGE == 0:
            g.plan.add("pool", (lambda t=t: (lambda e: e.dma_start(
                out=yout.rearrange("(n c p) d -> n p c d", p=128, c=CH)[t],
                in_=bufs["lntm"])))(), waits=(lntok,), inc=("st", 16))
            fr[("lntm",)] = g.plan.tok("st")
            fr[("lnfm",)] = g.plan.tok("pe")
            fr[("sbx", 0)] = fr[("sbx", 1)] = lntok
            continue
        if D_STAGE == 1:
            g.plan.add("pool", (lambda t=t: (lambda e: e.dma_start(
                out=yout.rearrange("(n c p) d -> n p c d", p=128, c=CH)[t],
                in_=x1tm)))(), waits=(x1tmtok,), inc=("st", 16))
            fr[("x1tm",)] = g.plan.tok("st")
            fr[("lnfm",)] = g.plan.tok("pe")
            fr[("xcbf",)] = g.plan.tok("pe")
            fr[("xc2",)] = g.plan.tok("dve")
            fr[("sbx", 0)] = fr[("sbx", 1)] = lntok
            continue
        for qc in range(CH):
            bank = 2 + (qc % 2)
            pairs = [(x1fm[:, dc, qc * 128:(qc + 1) * 128], rwT[:, dc, :])
                     for dc in range(KC)]
            petok = g.mm(bank, pairs, waits=[x1tok, wtok])
            c0 = _dv(g, (lambda bank=bank: (lambda e: e.tensor_tensor(
                lg, g.pb[bank][:, 0:8], rbb[:, 0:8], ALU.add)))(),
                     waits=(petok, bctok))
            g.bankfree[bank] = c0
            c1 = _dv(g, lambda e: e.reduce_max(m1, lg, axis=AX.X), waits=(c0,))
            c2 = _dv(g, lambda e: e.reduce_max(nm1, lg, axis=AX.X, negate=True), waits=(c0,))
            c3 = _dv(g, lambda e: e.tensor_scalar(msk1, lg, m1[:, 0:1], None, ALU.is_ge),
                     waits=(c1,))
            c4 = _dv(g, lambda e: e.scalar_tensor_tensor(lgm, msk1, -1e9, lg, ALU.mult, ALU.add),
                     waits=(c3,))
            c5 = _dv(g, lambda e: e.reduce_max(m2t, lgm, axis=AX.X), waits=(c4,))
            c6 = _dv(g, lambda e: e.tensor_scalar(msk2, lgm, m2t[:, 0:1], None, ALU.is_ge),
                     waits=(c5,))
            a1 = _ac(g, lambda e: e.activation(e8, lg, AF.Exp, bias=nm1[:, 0:1]),
                     waits=(c2,))
            c8 = _dv(g, lambda e: e.tensor_tensor(msk1, msk1, msk2, ALU.add), waits=(c6,))
            c9 = _dv(g, lambda e: e.tensor_tensor(em8, e8, msk1, ALU.mult), waits=(c8, a1))
            c10 = _dv(g, lambda e: e.reduce_sum(s8, em8, axis=AX.X), waits=(c9,))
            c11 = _dv(g, lambda e: e.reciprocal(rs, s8), waits=(c10,))
            c12 = _ac(g, (lambda qc=qc: (lambda e: e.activation(
                rw[:, qc, :], em8, AF.Copy, scale=rs[:, 0:1])))(),
                      waits=(c11,))
            if D_SAFE & 8:
                _ac(g, lambda e: e.activation(lgm, lg, AF.Silu), waits=(c12,))
        rwtok = g.plan.tok("act")
        if D_STAGE == 15:
            g.plan.add("sp", (lambda t=t: (lambda e: e.dma_start(
                out=yout.rearrange("(n c p) d -> n p c d", p=128, c=CH)[t][:, :, 0:8],
                in_=rw)))(), waits=(rwtok,), inc=("st", 16))
            fr[("rw",)] = g.plan.tok("st")
            fr[("lnfm",)] = g.plan.tok("pe")
            fr[("xcbf",)] = g.plan.tok("pe")
            fr[("xc2",)] = g.plan.tok("dve")
            fr[("x1tm",)] = g.plan.tok("pe")
            fr[("sbx", 0)] = fr[("sbx", 1)] = lntok
            continue
        combtok = None
        for e_i in range(8):
            ewtok = ("ld", g.loadc(ewe, g.H["D_ewT"][e_i].rearrange("(c p) o -> p c o", p=128),
                                   waits=(fr.get(("ewe",), ("ph", 0)),)))
            for qc in range(CH):
                for half in range(2):
                    bank = 2 + ((qc * 2 + half) % 4)
                    allw = [ewtok, x1tok, wtok, g.bankfree[bank]]
                    if D_SAFE & 1:
                        allw += [g.plan.tok("dve"), g.plan.tok("act")]
                    for dc in range(KC):
                        p.add("pe",
                              (lambda qc=qc, half=half, bank=bank, dc=dc: (
                                  lambda e: e.matmul(
                                      g.pb[bank][:, :512],
                                      x1fm[:, dc, qc * 128:(qc + 1) * 128],
                                      ewe[:, dc, half * 512:(half + 1) * 512],
                                      start=(dc == 0), stop=False)))(),
                              waits=tuple(allw) if dc == 0 else ())
                    p.add("pe",
                          (lambda e_i=e_i, half=half, bank=bank: (lambda e: e.matmul(
                              g.pb[bank][:, :512], onesbf,
                              ebrow[:, e_i * 1024 + half * 512:e_i * 1024 + (half + 1) * 512],
                              start=False, stop=True)))(),
                          waits=())
                    if D_SAFE & 4:
                        p.add("pe",
                              lambda e: e.matmul(g.pb[6][:, 0:128], onesbf,
                                                 onesbf, start=True, stop=True),
                              waits=(), inc=("pe", 1))
                    else:
                        g.plan.q["pe"][-1][2] = ("pe", 1)
                        g.plan.cnt["pe"] += 1
                    petok = g.plan.tok("pe")
                    sbuf_s = sil[(qc * 2 + half) % 2]
                    atok = _ac(g, (lambda bank=bank, sbuf_s=sbuf_s: (
                        lambda e: e.activation(sbuf_s, g.pb[bank][:, :512], AF.Silu)))(),
                               waits=(petok, fr.get(("sil", (qc * 2 + half) % 2), ("ph", 0))))
                    g.bankfree[bank] = atok
                    if e_i == 0 and "dbg_sil" in g.H:
                        g.plan.add("sp", (lambda t=t, qc=qc, half=half, sbuf_s=sbuf_s: (
                            lambda e: e.dma_start(out=g.H["dbg_sil"][t, qc, half],
                                                  in_=sbuf_s)))(),
                            waits=(atok,), inc=("st", 16))
                        atok = ("st", g.plan.cnt["st"])
                    if e_i == 0:
                        combtok = _dv(g, (lambda qc=qc, half=half, sbuf_s=sbuf_s, e_i=e_i: (
                            lambda e: e.tensor_scalar(
                                comb[:, qc, half * 512:(half + 1) * 512], sbuf_s,
                                rw[:, qc, e_i:e_i + 1], None, ALU.mult)))(),
                                      waits=(atok, rwtok))
                    else:
                        combtok = _dv(g, (lambda qc=qc, half=half, sbuf_s=sbuf_s, e_i=e_i: (
                            lambda e: e.scalar_tensor_tensor(
                                comb[:, qc, half * 512:(half + 1) * 512], sbuf_s,
                                rw[:, qc, e_i:e_i + 1],
                                comb[:, qc, half * 512:(half + 1) * 512],
                                ALU.mult, ALU.add)))(),
                                      waits=(atok, rwtok))
                    fr[("sil", (qc * 2 + half) % 2)] = combtok
            fr[("ewe",)] = g.plan.tok("pe")
        if D_STAGE == 16:
            g.plan.add("pool", (lambda t=t: (lambda e: e.dma_start(
                out=g.H["dbg_x1fm"][t], in_=lnfm3[:, :, 2:514])))(),
                waits=(g.plan.tok("pe"),), inc=("st", 16))
            fr[("comb",)] = g.plan.tok("st")
            fr[("lnfm",)] = g.plan.tok("st")
            fr[("xcbf",)] = g.plan.tok("pe")
            fr[("xc2",)] = g.plan.tok("dve")
            fr[("x1tm",)] = g.plan.tok("pe")
            fr[("sbx", 0)] = fr[("sbx", 1)] = lntok
            fr[("ewe",)] = g.plan.tok("pe")
            continue
        if D_STAGE == 2:
            g.plan.add("sp", (lambda t=t: (lambda e: e.dma_start(
                out=yout.rearrange("(n c p) d -> n p c d", p=128, c=CH)[t],
                in_=comb)))(), waits=(combtok,), inc=("st", 16))
            fr[("comb",)] = g.plan.tok("st")
            fr[("lnfm",)] = g.plan.tok("pe")
            fr[("xcbf",)] = g.plan.tok("pe")
            fr[("xc2",)] = g.plan.tok("dve")
            fr[("x1tm",)] = g.plan.tok("pe")
            fr[("sbx", 0)] = fr[("sbx", 1)] = lntok
            continue
        htok = None
        for qc in range(CH):
            cbtok = _dv(g, (lambda qc=qc: (lambda e: e.tensor_copy(
                combbf, comb[:, qc, :])))(),
                        waits=(combtok, fr.get(("combbf",), ("ph", 0))))
            for kg in range(2):
                srcs = [combbf[:, (kg * 4 + j) * 128:(kg * 4 + j + 1) * 128]
                        for j in range(4)]
                dsts = [hfm[:, kg * 4 + j, qc * 128:(qc + 1) * 128] for j in range(4)]
                tptok = g.transpose4(kg, srcs, dsts,
                                     waits=(cbtok, fr.get(("hfm",), ("ph", 0))),
                                     evac_eng="dve")
            rptok = _dv(g, (lambda qc=qc: (lambda e: e.tensor_tensor(
                rpb, x1tm[:, qc, :], obb, ALU.add)))(),
                        waits=(x1tmtok, bctok, fr.get(("rpb",), ("ph", 0))))
            utoks = []
            for half in range(2):
                bank = 2 + half
                pairs = [(hfm[:, dc, qc * 128:(qc + 1) * 128],
                          owT[:, dc, half * 512:(half + 1) * 512]) for dc in range(KC)]
                petok = g.mm(bank, pairs, waits=[tptok, wtok])
                ut = _dv(g, (lambda qc=qc, half=half, bank=bank: (
                    lambda e: e.scalar_tensor_tensor(
                        comb[:, qc, half * 512:(half + 1) * 512],
                        g.pb[bank][:, :512], 1.0,
                        rpb[:, half * 512:(half + 1) * 512], ALU.mult, ALU.add)))(),
                             waits=(petok, rptok))
                g.bankfree[bank] = ut
                utoks.append(ut)
            fr[("rpb",)] = utoks[-1]
            sstok = p.add(
                "act",
                (lambda qc=qc: (lambda e: e.activation(
                    combbf, comb[:, qc, :], AF.Square,
                    accum_out=ss1[:, 0:1])))(),
                waits=tuple(utoks) + (g.plan.tok("pe"),), inc=("act", 1))
            fr[("combbf",)] = ("act", sstok)
            s2tok = _dv(g, lambda e: e.tensor_scalar(ss2, ss1, 1.0 / D, 1e-6, ALU.mult, ALU.add),
                        waits=(("act", sstok),))
            sqtok = _ac(g, lambda e: e.activation(rmsv, ss2, AF.Sqrt), waits=(s2tok,))
            rrtok = _dv(g, (lambda qc=qc: (lambda e: e.reciprocal(
                rrms4[:, qc:qc + 1], rmsv)))(), waits=(sqtok,))
            htok = _dv(g, (lambda qc=qc: (lambda e: e.scalar_tensor_tensor(
                hbf[:, qc, :], comb[:, qc, :], rrms4[:, qc:qc + 1], nwb,
                ALU.mult, ALU.mult)))(),
                       waits=(rrtok, bctok, fr.get(("hbf",), ("ph", 0))))
        if D_STAGE == 3:
            g.plan.add("pool", (lambda t=t: (lambda e: e.dma_start(
                out=yout.rearrange("(n c p) d -> n p c d", p=128, c=CH)[t],
                in_=hbf)))(), waits=(htok,), inc=("st", 16))
            fr[("hbf",)] = g.plan.tok("st")
            fr[("lnfm",)] = g.plan.tok("pe")
            fr[("xcbf",)] = g.plan.tok("pe")
            fr[("xc2",)] = g.plan.tok("dve")
            fr[("x1tm",)] = g.plan.tok("pe")
            fr[("rpb",)] = g.plan.tok("dve")
            fr[("combbf",)] = g.plan.tok("act")
            fr[("sbx", 0)] = fr[("sbx", 1)] = lntok
            continue
        moetok = g.plan.tok("pe")
        hfmtok = None
        for qc in range(CH):
            for kg in range(2):
                srcs = [hbf[:, qc, (kg * 4 + j) * 128:(kg * 4 + j + 1) * 128]
                        for j in range(4)]
                dsts = [hfm[:, kg * 4 + j, qc * 128:(qc + 1) * 128] for j in range(4)]
                hfmtok = g.transpose4(kg, srcs, dsts, waits=(htok, moetok),
                                      evac_eng="dve")
        fftok = None
        for oc in range(2):
            bank = 2 + oc
            pairs = [(ffdT[:, dc, oc * 128:(oc + 1) * 128], hfm[:, dc, :])
                     for dc in range(KC)]
            petok = g.mm(bank, pairs, waits=[hfmtok, wtok])
            fftok = p.add(
                "act",
                (lambda oc=oc, bank=bank: (lambda e: e.activation(
                    ff1[:, oc, :], g.pb[bank][:, :TT], AF.Gelu,
                    bias=ffdb[:, oc:oc + 1])))(),
                waits=(petok,), inc=("act", 1))
            g.bankfree[bank] = ("act", fftok)
        if D_STAGE == 4:
            g.plan.add("pool", (lambda t=t: (lambda e: e.dma_start(
                out=yout.rearrange("(n c p) d -> n p c d", p=128, c=CH)[t][:, 0:2, 0:512],
                in_=ff1)))(), waits=(("act", fftok),), inc=("st", 16))
            fr[("ff1",)] = g.plan.tok("st")
            fr[("lnfm",)] = g.plan.tok("pe")
            fr[("xcbf",)] = g.plan.tok("pe")
            fr[("xc2",)] = g.plan.tok("dve")
            fr[("x1tm",)] = g.plan.tok("pe")
            fr[("rpb",)] = g.plan.tok("dve")
            fr[("combbf",)] = g.plan.tok("act")
            fr[("hbf",)] = g.plan.tok("pe")
            fr[("hfm",)] = g.plan.tok("pe")
            fr[("sbx", 0)] = fr[("sbx", 1)] = lntok
            continue
        outtok = None
        for qc in range(CH):
            h1 = _dv(g, (lambda qc=qc: (lambda e: e.scalar_tensor_tensor(
                hf32, comb[:, qc, :], rrms4[:, qc:qc + 1], nwb,
                ALU.mult, ALU.mult)))(), waits=(htok, fr.get(("hf32",), ("ph", 0))))
            h2 = _dv(g, (lambda qc=qc, t=t: (lambda e: e.tensor_tensor(
                hf32, hf32, ffubb, ALU.add)))(), waits=(h1,))
            h3 = _dv(g, (lambda qc=qc, t=t: (lambda e: e.tensor_tensor(
                hf32, hf32, bufs["sbx"][t % 2][:, qc, :], ALU.add)))(),
                     waits=(h2, xtok))
            for half in range(2):
                bank = 4 + half
                pairs = [(ff1[:, fc, qc * 128:(qc + 1) * 128],
                          ffuT[:, fc, half * 512:(half + 1) * 512]) for fc in range(2)]
                petok = g.mm(bank, pairs, waits=[("act", fftok), wtok])
                outtok = _dv(g, (lambda qc=qc, half=half, bank=bank, t=t: (
                    lambda e: e.tensor_tensor(
                        bufs["sbx"][t % 2][:, qc, half * 512:(half + 1) * 512],
                        g.pb[bank][:, :512],
                        hf32[:, half * 512:(half + 1) * 512], ALU.add)))(),
                             waits=(petok, h3))
                g.bankfree[bank] = outtok
            fr[("hf32",)] = outtok
        sttok = g.store(yout.rearrange("(n c p) d -> n p c d", p=128, c=CH)[t],
                        bufs["sbx"][t % 2], waits=(outtok,))
        fr[("sbx", 0)] = ("st", sttok)
        fr[("sbx", 1)] = ("st", sttok)
        fr[("hfm",)] = g.plan.tok("pe")
        fr[("hbf",)] = g.plan.tok("pe")
        fr[("lnfm",)] = g.plan.tok("pe")
        fr[("xcbf",)] = g.plan.tok("pe")
        fr[("xc2",)] = g.plan.tok("dve")
        fr[("x1tm",)] = g.plan.tok("pe")
        if D_SAFE & 2:
            p.barrier()



def run_group(build_fns, io_spec, in_maps_extra, core_ids=None, trace=False):
    from contextlib import ExitStack
    if core_ids is None:
        core_ids = list(range(NCORES))
    nc = bass.Bass("TRN2", target_bir_lowering=False, debug=False)
    with ExitStack() as stack:
        g = G(nc, stack)
        for name, shape, dtype, kind in io_spec:
            g.dram(name, shape, dtype, kind)
        for fn in build_fns:
            fn(g)
        sems = {}
        for s in g.plan.sem_names:
            sems[s] = stack.enter_context(nc.semaphore(s))
        replay(nc, g.plan, sems)
    res = run_bass_kernel_spmd(nc, in_maps_extra, core_ids=core_ids, trace=trace)
    return res


A_KEYS = ["A_qkvw", "A_qkvb", "A_rqw", "A_rqb", "A_rkvw", "A_rkvb", "A_routw",
          "A_outw", "A_vb_row", "A_rvb_row", "A_routb_row", "A_outb_row",
          "A_glw", "A_grw"]
A_CONSTS = ["c_ident", "c_ones1", "c_triu", "c_trils", "c_latmask", "c_pool",
            "c_zvec1", "c_zvec64"]
B_KEYS = ["B_gwT", "B_w1", "B_b1", "B_w2", "B_b2"]
B_CONSTS = ["c_ident", "c_identf", "c_ones1"]
C_KEYS = ["C_wq", "C_wk", "C_wv"]
C_CONSTS = ["c_ident", "c_ones1"]
D_KEYS = ["D_dwk", "D_dbf", "D_pww", "D_pwbf", "D_rwT", "D_rb", "D_ewT",
          "D_ebrow", "D_owT", "D_ob", "D_nw", "D_ffdT", "D_ffdbf", "D_ffuT",
          "D_ffub"]
D_CONSTS = ["c_ident", "c_ones1"]


def _io_for(sh, keys):
    out = []
    for k in keys:
        a = sh[k]
        dt = BF16 if a.dtype == ml_dtypes.bfloat16 else F32
        out.append((k, list(a.shape), dt, "ExternalInput"))
    return out


def _launch(sh, build_fn, io_spec, per_core_extra):
    in_maps = []
    for c in range(NCORES):
        m = dict(per_core_extra[c])
        for name, _, _, kind in io_spec:
            if kind == "ExternalInput" and name not in m:
                m[name] = sh[name]
        in_maps.append(m)
    return run_group([build_fn], io_spec, in_maps)


def kernel(**inputs):
    sh, xs = host_prepare(inputs)

    io = [("x0", [S, D], F32, "ExternalInput"),
          ("rA", [S, D], F32, "ExternalOutput")] + _io_for(sh, A_KEYS + A_CONSTS)

    def bA(g):
        emit_consts(g)
        build_phase_A(g)
    res = _launch(sh, bA, io, [{"x0": xs[c]} for c in range(NCORES)])
    rA = [np.asarray(res.results[c]["rA"]) for c in range(NCORES)]

    io = [("rA", [S, D], F32, "ExternalInput"),
          ("rB", [S, D], F32, "ExternalOutput")] + _io_for(sh, B_KEYS + B_CONSTS)

    def bB(g):
        emit_consts(g)
        build_phase_B(g)
    res = _launch(sh, bB, io, [{"rA": rA[c]} for c in range(NCORES)])
    rB = [np.asarray(res.results[c]["rB"]) for c in range(NCORES)]

    io = [("rB", [S, D], F32, "ExternalInput"),
          ("rC", [S, D], F32, "ExternalOutput")] + _io_for(sh, C_KEYS + C_CONSTS)

    def bC(g):
        emit_consts(g)
        build_phase_C(g)
    res = _launch(sh, bC, io, [{"rB": rB[c]} for c in range(NCORES)])
    rC = [np.asarray(res.results[c]["rC"]) for c in range(NCORES)]

    io = [("rC", [S, D], F32, "ExternalInput"),
          ("y", [S, D], F32, "ExternalOutput")] + _io_for(sh, D_KEYS + D_CONSTS)

    def bD(g):
        emit_consts(g)
        build_phase_D(g)
    res = _launch(sh, bD, io, [{"rC": rC[c]} for c in range(NCORES)])
    ys = [np.asarray(res.results[c]["y"]) for c in range(NCORES)]
    return np.stack(ys, axis=0)
